# version 23
# baseline (speedup 1.0000x reference)
import math
from contextlib import ExitStack

import numpy as np
import ml_dtypes

import concourse.bass as bass
import concourse.mybir as mybir
from concourse.bass_utils import run_bass_kernel_spmd

F32 = mybir.dt.float32
BF16 = mybir.dt.bfloat16
AF = mybir.ActivationFunctionType
ALU = mybir.AluOpType
AX = mybir.AxisListType
BF = ml_dtypes.bfloat16

P = 128
DM = 4096
NCH = DM // P
S = 2048
B = 4
H = 32
G = 8
D = 128
TOK = 1024
NKC = S // P

ENGINES = ("tensor", "scalar", "vector", "gpsimd", "sync")

Q0 = [128 * (kc // 2) for kc in range(NKC)]
W_KC = [TOK - q0 for q0 in Q0]
OFF_KC = [int(x) for x in np.concatenate([[0], np.cumsum(W_KC)])]
PT_W = OFF_KC[-1]


def _pieces(c0, c1, align=512):
    out = []
    c = c0
    while c < c1:
        nxt = min(c1, (c // align + 1) * align)
        out.append((c, nxt))
        c = nxt
    return out


class Prog:

    def __init__(self, nc, sems):
        self.nc = nc
        self.sems = sems
        self.q = {e: [] for e in ENGINES}
        self.c = {k: 0 for k in sems}
        self.serialize = False

    def emit(self, eng, fn, waits=(), inc=None, inc_n=1):
        if self.serialize:
            waits = tuple(waits) + tuple((k, v) for k, v in self.c.items() if v > 0)
        waits = tuple(waits)
        for s, v in waits:
            assert v <= self.c[s], f"wait on {s}>={v} but only {self.c[s]} scheduled"
        sems = self.sems

        def run(e, fn=fn, waits=waits, inc=inc, inc_n=inc_n):
            for s, v in waits:
                e.wait_ge(sems[s], v)
            r = fn(e)
            if inc is not None:
                r.then_inc(sems[inc], inc_n)
        self.q[eng].append(run)
        if inc is not None:
            self.c[inc] += inc_n
        return self.c[inc] if inc is not None else None

    def join_all(self):
        snap = [(k, v) for k, v in self.c.items() if v > 0]
        for e in ENGINES:
            if e == "gpsimd":
                continue
            for k, v in snap:
                self.wait(e, k, v)

    def wait(self, eng, sname, val):
        assert val <= self.c[sname], f"wait on {sname}>={val} but only {self.c[sname]}"
        sems = self.sems
        self.q[eng].append(lambda e, s=sname, v=val: e.wait_ge(sems[s], v))

    def flush(self, name):
        with self.nc.named_scope(name), self.nc.Block(name) as block:
            for e in ENGINES:
                ops = self.q[e]
                if not ops:
                    continue
                getattr(block, e)(lambda engine, ops=tuple(ops): [op(engine) for op in ops])
        self.q = {e: [] for e in ENGINES}


def build_nc(taps=()):
    nc = bass.Bass("TRN2", target_bir_lowering=False)

    def dram_in(name, shape, dt):
        return nc.dram_tensor(name, shape, dt, kind="ExternalInput").ap()

    def dram_scratch(name, shape, dt):
        if name in taps:
            return nc.dram_tensor(name, shape, dt, kind="ExternalOutput").ap()
        return nc.dram_tensor(name, shape, dt).ap()

    xsel = dram_in("xsel", [DM, TOK], F32)
    wq_t = dram_in("wq_t", [16, P, 32, 256], F32)
    wk_t = dram_in("wk_t", [4, P, 32, 256], F32)
    wv_t = dram_in("wv_t", [4, P, 32, 256], F32)
    wo_t = dram_in("wo_t", [16, P, 32, 256], F32)
    kn_t = dram_in("kn_t", [16, P, 32, 256], F32)
    bq_r = dram_in("bq_r", [P, 32], F32)
    bk_r = dram_in("bk_r", [P, 8], F32)
    bv_r = dram_in("bv_r", [P, 8], F32)
    bo_r = dram_in("bo_r", [P, 32], F32)
    bi_r = dram_in("bi_r", [P, 32], F32)
    g2 = dram_in("g2", [16, 2], F32)
    la2 = dram_in("la2", [512, DM], F32)
    lbt = dram_in("lbt", [512, DM], F32)
    asht = dram_in("asht", [DM, 32], F32)
    bsht = dram_in("bsht", [32, DM], F32)
    cq_h = dram_in("cq_h", [P, TOK], BF16)
    sq_h = dram_in("sq_h", [P, TOK], BF16)
    ckq_h = dram_in("ckq_h", [P, TOK], BF16)
    skq_h = dram_in("skq_h", [P, TOK], BF16)
    mk_h = dram_in("mk_h", [NKC, P, P], BF16)
    esel_h = dram_in("esel_h", [16, 512], F32)
    i32s_h = dram_in("i32s_h", [P, 32], BF16)
    i128b_h = dram_in("i128b_h", [P, P], BF16)
    i128f_h = dram_in("i128f_h", [P, P], F32)

    qbuf = dram_scratch("qbuf", [DM, TOK], BF16)
    cc_in = nc.dram_tensor("cc_in", [P, 2 * G * TOK], BF16)
    cc_out = nc.dram_tensor("cc_out", [2 * P, 2 * G * TOK], BF16)
    z1buf = dram_scratch("z1buf", [DM, TOK], F32)
    y1buf = dram_scratch("y1buf", [DM, TOK], F32)
    z2buf = dram_scratch("z2buf", [DM, TOK], F32)
    outT = nc.dram_tensor("outT", [DM, TOK], F32, kind="ExternalOutput").ap()

    tap_kt = nc.dram_tensor("tap_kt", [P, G * S], BF16, kind="ExternalOutput").ap() if "kt" in taps else None
    tap_vt = nc.dram_tensor("tap_vt", [P, G * S], BF16, kind="ExternalOutput").ap() if "vt" in taps else None
    tap_ctx = nc.dram_tensor("tap_ctx", [P, NCH * TOK], BF16, kind="ExternalOutput").ap() if "ctx" in taps else None
    tap_ax = nc.dram_tensor("tap_ax", [P, TOK], BF16, kind="ExternalOutput").ap() if "ax" in taps else None
    tap_acomb = nc.dram_tensor("tap_acomb", [P, NCH * 128], BF16, kind="ExternalOutput").ap() if "ax" in taps else None
    tap_bstack = nc.dram_tensor("tap_bstack", [P, DM], BF16, kind="ExternalOutput").ap() if "ax" in taps else None

    stack = ExitStack()
    with stack:
        def sb(st, name, shape, dt):
            return st.enter_context(nc.sbuf_tensor(name, shape, dt))

        def ps(st, name, shape, dt=F32):
            return st.enter_context(nc.psum_tensor(name, shape, dt))

        bq_s = sb(stack, "bq_s", [P, 32], F32)
        bk_s = sb(stack, "bk_s", [P, 8], F32)
        bv_s = sb(stack, "bv_s", [P, 8], F32)
        bo_s = sb(stack, "bo_s", [P, 32], F32)
        bi_s = sb(stack, "bi_s", [P, 32], F32)
        g2_s = sb(stack, "g2_s", [16, 2], F32)
        esel_s = sb(stack, "esel_s", [16, 512], F32)
        i32s_s = sb(stack, "i32s_s", [P, 32], BF16)
        i128b_s = sb(stack, "i128b_s", [P, P], BF16)
        i128f_s = sb(stack, "i128f_s", [P, P], F32)
        ones128b = sb(stack, "ones128b", [P, P], BF16)
        ones1f = sb(stack, "ones1f", [1, P], F32)
        onescb = sb(stack, "onescb", [P, 1], BF16)
        zo2 = sb(stack, "zo2", [P, 2], BF16)
        st2_sb = sb(stack, "st2_sb", [2, TOK], F32)
        s2row = sb(stack, "s2row", [1, TOK], F32)
        A_comb = sb(stack, "A_comb", [P, NCH * 128], BF16)
        B_stack = sb(stack, "B_stack", [P, DM], BF16)
        Ax_sb = sb(stack, "Ax_sb", [P, TOK], BF16)
        gdelta_s = sb(stack, "gdelta_s", [P, 8 * 32], BF16)
        gr_sb = sb(stack, "gr_sb", [P, 1], F32)
        mneg = sb(stack, "mneg", [P, 1], F32)
        mx1 = sb(stack, "mx1", [32, 1], F32)
        mxs = sb(stack, "mxs", [1, 1], F32)

        sm = {k: stack.enter_context(nc.semaphore(k))
              for k in ("T", "Sc", "V", "Pl", "LG", "LY", "SQ", "CC",
                        "N0", "N1", "N2", "R0", "R1", "W0", "W1", "W2", "W3",
                        "X0", "X1", "X2", "X3", "M0", "M1", "M2", "QA", "QB")}
        pr = Prog(nc, sm)

        WT = {}
        import os
        _ser = os.environ.get("KSER", "")
        

        def make_ws(wdram, nfg, init_gate=None):
            return {"wdram": wdram, "nfg": nfg, "nbuf": len(WT["bufs"]),
                    "entry_gate": pr.c["T"] if init_gate is None else init_gate,
                    "loads": {}, "t_unit": {}}

        def ws_load(ws, u, gate=None):
            if u in ws["loads"]:
                return ws["loads"][u]
            nbuf = ws["nbuf"]
            if gate is not None:
                pr.wait("gpsimd", "T", gate)
            elif u >= nbuf:
                pr.wait("gpsimd", "T", ws["t_unit"][u - nbuf])
            else:
                pr.wait("gpsimd", "T", ws["entry_gate"])
            fg, hf = u // 2, u % 2
            wsem = f"W{u % nbuf}"
            mrk = pr.emit("gpsimd", lambda e, hf=hf, fg=fg, u=u, nbuf=nbuf, wdram=ws["wdram"]: e.dma_start(
                WT["bufs"][u % nbuf][:, :].rearrange("p (d c) -> p d c", c=256),
                wdram[fg, :, hf * 16:(hf + 1) * 16, :]),
                inc=wsem, inc_n=16)
            ws["loads"][u] = (wsem, mrk)
            return ws["loads"][u]

        def stream_gemm(ws, get_psum, rhs_marks, first_fg_waits,
                        plain_stop, on_fg_done, mid_cb=None):
            t_unit = ws["t_unit"]
            t_fg = {}
            nfg = ws["nfg"]
            nbuf = ws["nbuf"]
            for fg in range(nfg):
                for hf in range(2):
                    u = 2 * fg + hf
                    wsem, mrk = ws_load(ws, u)
                    if u == 1 and mid_cb is not None:
                        mid_cb()
                    for gi in range(2):
                        if gi == 0:
                            pr.wait("tensor", wsem, mrk)
                        if rhs_marks is not None:
                            rs, rv = rhs_marks[hf * 16 + gi * 8 + 7]
                            pr.wait("tensor", rs, rv)
                        if hf == 0 and gi == 0:
                            for s_, v_ in first_fg_waits(fg):
                                pr.wait("tensor", s_, v_)
                        for dd in range(gi * 8, gi * 8 + 8):
                            d = hf * 16 + dd
                            for fsub in range(2):
                                for nh in range(2):
                                    last_u = (dd == 15 and fsub == 1 and nh == 1)
                                    last_fg = (d == NCH - 1 and fsub == 1 and nh == 1)
                                    r = pr.emit("tensor", lambda e, fg=fg, u=u, dd=dd, d=d, fsub=fsub, nh=nh, nbuf=nbuf: e.matmul(
                                        get_psum(fg, fsub)[:, nh * 512:(nh + 1) * 512],
                                        WT["bufs"][u % nbuf][:, dd * 256 + fsub * 128: dd * 256 + fsub * 128 + 128],
                                        big0[:, d * TOK + nh * 512: d * TOK + (nh + 1) * 512],
                                        start=(d == 0),
                                        stop=(d == NCH - 1 and plain_stop)),
                                        inc="T" if last_u else None)
                                    if last_u:
                                        t_unit[u] = r
                                    if last_fg:
                                        t_fg[fg] = r
                on_fg_done(fg, t_fg[fg])
            return t_fg

        pr.serialize = "b0" in _ser
        ly_consts = None
        for dst, src in ((bq_s, bq_r), (bk_s, bk_r), (bv_s, bv_r), (bo_s, bo_r),
                         (bi_s, bi_r), (g2_s, g2), (esel_s, esel_h), (i32s_s, i32s_h),
                         (i128b_s, i128b_h), (i128f_s, i128f_h)):
            ly_consts = pr.emit("sync", lambda e, d=dst, s=src: e.dma_start(d[:, :], s),
                                inc="LY", inc_n=16)
        pr.emit("gpsimd", lambda e: e.memset(ones128b[:, :], 1.0))
        pr.emit("gpsimd", lambda e: e.memset(ones1f[:, :], 1.0))
        pr.emit("gpsimd", lambda e: e.memset(onescb[:, :], 1.0))
        pr.emit("gpsimd", lambda e: e.memset(zo2[:, 0:1], 0.0))
        pr.emit("gpsimd", lambda e: e.memset(zo2[:, 1:2], 1.0), inc="Pl")
        pr.emit("gpsimd", lambda e: e.dma_start(
            A_comb[:, :].rearrange("p (ch c) -> p ch c", c=128)[:, :, 32:64],
            asht.rearrange("(ch p) r -> p ch r", p=P)), inc="LG", inc_n=16)
        lg_asht = pr.emit("gpsimd", lambda e: e.dma_start(
            A_comb[:, :].rearrange("p (ch c) -> p ch c", c=128)[:, :, 96:128],
            asht.rearrange("(ch p) r -> p ch r", p=P)), inc="LG", inc_n=16)
        pr.emit("gpsimd", lambda e: e.dma_start(B_stack[32:64, :], bsht),
                inc="LG", inc_n=16)
        lg_bsht = pr.emit("gpsimd", lambda e: e.dma_start(B_stack[96:128, :], bsht),
                          inc="LG", inc_n=16)

        scope_big = ExitStack()
        big0 = sb(scope_big, "big0", [P, NCH * TOK], BF16)
        scope_kv = ExitStack()
        kT = sb(scope_kv, "kT", [P, G * S], BF16)
        vT = sb(scope_kv, "vT", [P, G * S], BF16)
        scope_1ab = ExitStack()
        WT["bufs"] = [sb(scope_1ab, f"wta{i}", [P, 4096], BF16) for i in range(3)]
        sgA = [sb(scope_1ab, f"sgA{i}", [P, TOK], BF16) for i in range(2)]

        scope_b0 = ExitStack()
        lw = [sb(scope_b0, f"lw{i}", [P, 2048], BF16) for i in range(4)]
        ae_sb = sb(scope_b0, "ae_sb", [32, 2048], BF16)
        gr_ps = ps(scope_b0, "gr_ps", [P, 1])
        ae_ps = ps(scope_b0, "ae_ps", [32, 2048])
        aet_ps = ps(scope_b0, "aet_ps", [P, 32], BF16)

        pr.serialize = True
        v_prev = 0
        for vb in range(2):
            for a in range(4):
                t_gr = pr.emit("tensor", lambda e, a=a, vb=vb: e.matmul(
                    gr_ps[:, :], esel_s[:, a * P:(a + 1) * P], g2_s[:, vb:vb + 1],
                    start=True, stop=True),
                    waits=[("LY", ly_consts), ("V", v_prev)], inc="T")
                v_gr = pr.emit("vector", lambda e: e.tensor_copy(gr_sb[:, :], gr_ps[:, :]),
                               waits=[("T", t_gr)], inc="V")
                v_prev = pr.emit("vector", lambda e, vb=vb, a=a: e.tensor_scalar(
                    gdelta_s[:, (vb * 4 + a) * 32:(vb * 4 + a + 1) * 32],
                    i32s_s[:, :], gr_sb[:, :], None, ALU.mult), inc="V")
        v_gd_done = v_prev
        pr.serialize = False

        b_last_mark = None
        for branch, srcdram in (("A", la2), ("B", lbt)):
            for half in range(2):
                lg_marks = []
                pr.wait("gpsimd", "T", pr.c["T"])
                for rc in range(4):
                    lg = pr.emit("gpsimd", lambda e, rc=rc, half=half, sd=srcdram: e.dma_start(
                        lw[rc][:, :], sd[rc * P:(rc + 1) * P, half * 2048:(half + 1) * 2048]),
                        inc=f"X{rc}", inc_n=16)
                    lg_marks.append((f"X{rc}", lg))
                for vb in range(2):
                    t_last = None
                    for rc in range(4):
                        for ccp in range(4):
                            w_list = [lg_marks[rc]]
                            if rc == 0:
                                w_list += [("V", v_gd_done), ("Sc", pr.c["Sc"])]
                            t_last = pr.emit("tensor", lambda e, vb=vb, rc=rc, ccp=ccp: e.matmul(
                                ae_ps[:, ccp * 512:(ccp + 1) * 512],
                                gdelta_s[:, (vb * 4 + rc) * 32:(vb * 4 + rc + 1) * 32],
                                lw[rc][:, ccp * 512:(ccp + 1) * 512],
                                start=(rc == 0), stop=(rc == 3)),
                                waits=w_list if ccp == 0 else [],
                                inc="T" if (rc == 3 and ccp == 3) else None)
                    if branch == "B":
                        r0 = 0 if vb == 0 else 64
                        w_c = [("T", t_last)]
                        if b_last_mark is not None:
                            w_c.append(("LY", b_last_mark))
                        sc_b = pr.emit("scalar", lambda e: e.copy(ae_sb[:, :], ae_ps[:, :]),
                                       waits=w_c, inc="Sc")
                        b_last_mark = pr.emit("sync", lambda e, r0=r0, half=half: e.dma_start(
                            B_stack[r0:r0 + 32, half * 2048:(half + 1) * 2048], ae_sb[:, :]),
                            waits=[("Sc", sc_b)], inc="LY", inc_n=16)
                    else:
                        sc_a = pr.emit("scalar", lambda e: e.copy(ae_sb[:, :], ae_ps[:, :]),
                                       waits=[("T", t_last)], inc="Sc")
                        voff = 0 if vb == 0 else 64
                        for chh in range(16):
                            ch = half * 16 + chh
                            t_tp = pr.emit("tensor", lambda e, chh=chh: e.transpose(
                                aet_ps[:, :], ae_sb[:, chh * P:(chh + 1) * P], i128b_s[0:32, 0:32]),
                                waits=[("Sc", sc_a), ("V", pr.c["V"])], inc="T")
                            pr.emit("vector", lambda e, ch=ch, voff=voff: e.tensor_copy(
                                A_comb[:, ch * 128 + voff: ch * 128 + voff + 32], aet_ps[:, :]),
                                waits=[("T", t_tp)], inc="V")
        pr.serialize = False

        lg_xfb = {}

        def _xsel_load(g):
            xs = f"X{g}"
            mk = pr.emit("gpsimd", lambda e, g=g: e.dma_start(
                big0[:, g * 8 * TOK:(g + 1) * 8 * TOK].rearrange(
                    "p (c n) -> p c n", c=8),
                xsel[g * 8 * P:(g + 1) * 8 * P, :].rearrange(
                    "(c p) n -> p c n", p=P)),
                inc=xs, inc_n=16)
            for d in range(8 * g, 8 * g + 8):
                lg_xfb[d] = (xs, mk)

        ws_q = make_ws(wq_t, 16)
        pr.wait("gpsimd", "T", pr.c["T"])
        _xsel_load(0)
        _xsel_load(1)
        ws_load(ws_q, 0)
        ws_load(ws_q, 1)
        _xsel_load(2)
        _xsel_load(3)
        ws_load(ws_q, 2)

        pr.flush("b0_prep")
        pr.join_all()
        scope_b0.close()
        scope_sg = ExitStack()
        swp = [sb(scope_sg, f"swp{i}", [P, TOK], BF16) for i in range(2)]
        tmpr = [sb(scope_sg, f"tmpr{i}", [P, TOK], BF16) for i in range(2)]

        scope_p1a = ExitStack()
        sgB = [sb(scope_p1a, f"sgB{i}", [P, TOK], BF16) for i in range(2)]
        cq_s = sb(scope_p1a, "cq_s", [P, TOK], BF16)
        sq_s = sb(scope_p1a, "sq_s", [P, TOK], BF16)
        qp = [ps(scope_p1a, f"qp{i}", [P, TOK]) for i in range(4)]

        ly_trig = None
        for dst, src in ((cq_s, cq_h), (sq_s, sq_h)):
            ly_trig = pr.emit("sync", lambda e, d=dst, s=src: e.dma_start(d[:, :], s),
                              inc="LY", inc_n=16)

        sc_evac = {}
        rope_done = {}
        sq_store = {}

        def p1a_first_waits(fg):
            if fg >= 2:
                return [("V", sc_evac[2 * (fg - 2) + 1])]
            return []

        def p1a_fg_done(fg, t_mark):
            for fsub in range(2):
                fc = 2 * fg + fsub
                pb = fc % 2
                w_list = [("T", t_mark)]
                if fc >= 2:
                    w_list.append(rope_done[fc - 2])
                sc_evac[fc] = pr.emit("vector", lambda e, fg=fg, fsub=fsub, fc=fc, pb=pb: e.tensor_scalar(
                    sgA[pb][:, :], qp[2 * (fg % 2) + fsub][:, :],
                    bq_s[:, fc:fc + 1], None, ALU.add), waits=w_list, inc="V")
                w_sw = [("V", sc_evac[fc])]
                if fc >= 2:
                    w_sw.append(rope_done[fc - 2])
                pr.emit("sync", lambda e, pb=pb: e.dma_start(
                    swp[pb][0:64, :], sgA[pb][64:128, :]), waits=w_sw, inc="LY", inc_n=16)
                sw_mark = pr.emit("sync", lambda e, pb=pb: e.dma_start(
                    swp[pb][64:128, :], sgA[pb][0:64, :]), inc="LY", inc_n=16)
                eng, esem = ("vector", "V")
                w_list = [("V", sc_evac[fc]), ("LY", sw_mark)]
                if fc < 2:
                    w_list.append(("LY", ly_trig))
                else:
                    w_list.append(sq_store[fc - 2])
                pr.emit(eng, lambda e, pb=pb: e.tensor_tensor(
                    tmpr[pb][:, :], swp[pb][:, :], sq_s[:, :], ALU.mult), waits=w_list)
                pr.emit(eng, lambda e, pb=pb: e.tensor_tensor(
                    sgB[pb][:, :], sgA[pb][:, :], cq_s[:, :], ALU.mult))
                rope_done[fc] = (esem, pr.emit(eng, lambda e, pb=pb: e.tensor_tensor(
                    sgB[pb][:, :], sgB[pb][:, :], tmpr[pb][:, :], ALU.add), inc=esem))
                qs = "QA" if fc % 2 == 0 else "QB"
                sq_store[fc] = (qs, pr.emit("sync", lambda e, fc=fc, pb=pb: e.dma_start(
                    qbuf[fc * P:(fc + 1) * P, :], sgB[pb][:, :]),
                    waits=[rope_done[fc]], inc=qs, inc_n=16))

        stream_gemm(ws_q, lambda fg, fsub: qp[2 * (fg % 2) + fsub],
                    lg_xfb, p1a_first_waits, True, p1a_fg_done)
        ws_k = make_ws(wk_t, 4)
        for u in range(3):
            ws_load(ws_k, u, gate=ws_q["t_unit"][29 + (u + 1) % 3])
        pr.flush("p1a_q")
        pr.join_all()
        scope_p1a.close()
        t_after_p1a = pr.c["T"]

        scope_p1b = ExitStack()
        ckq_s = sb(scope_p1b, "ckq_s", [P, TOK], BF16)
        skq_s = sb(scope_p1b, "skq_s", [P, TOK], BF16)
        sgB2 = [sb(scope_p1b, f"sgB2{i}", [P, TOK], BF16) for i in range(2)]
        kv01 = [ps(scope_p1b, f"kv{i}", [P, TOK]) for i in range(2)]
        scope_k23 = ExitStack()
        kv23 = [ps(scope_k23, f"kv{i+2}", [P, TOK]) for i in range(2)]
        kv4 = kv01 + kv23

        ly_trig = None
        for dst, src in ((ckq_s, ckq_h), (skq_s, skq_h)):
            ly_trig = pr.emit("sync", lambda e, d=dst, s=src: e.dma_start(d[:, :], s),
                              inc="LY", inc_n=16)

        cc_marks = []

        def kv_evac_rope(kv, fg, t_mark, get_ps, evacs):
            for fsub in range(2):
                g = 2 * fg + fsub
                pb = g % 2
                if kv == 0:
                    w_list = [("T", t_mark), ("V", pr.c["V"]), ("LY", pr.c["LY"])]
                    sc_k = pr.emit("vector", lambda e, fg=fg, fsub=fsub, g=g, pb=pb: e.tensor_scalar(
                        sgA[pb][:, :], get_ps(fg, fsub)[:, :],
                        bk_s[:, g:g + 1], None, ALU.add), waits=w_list, inc="V")
                    evacs[2 * fg + fsub] = sc_k
                    pr.emit("sync", lambda e, pb=pb: e.dma_start(
                        swp[pb][0:64, :], sgA[pb][64:128, :]),
                        waits=[("V", sc_k)], inc="LY", inc_n=16)
                    sw_mark = pr.emit("sync", lambda e, pb=pb: e.dma_start(
                        swp[pb][64:128, :], sgA[pb][0:64, :]), inc="LY", inc_n=16)
                    pr.emit("vector", lambda e, pb=pb: e.tensor_tensor(
                        tmpr[pb][:, :], swp[pb][:, :], skq_s[:, :], ALU.mult),
                        waits=[("LY", sw_mark), ("LY", ly_trig)])
                    pr.emit("vector", lambda e, pb=pb: e.tensor_tensor(
                        sgB2[pb][:, :], sgA[pb][:, :], ckq_s[:, :], ALU.mult))
                    v_r = pr.emit("vector", lambda e, pb=pb: e.tensor_tensor(
                        sgB2[pb][:, :], sgB2[pb][:, :], tmpr[pb][:, :], ALU.add), inc="V")
                    cc_marks.append(pr.emit("sync", lambda e, g=g, pb=pb: e.dma_start(
                        cc_in.ap()[:, g * TOK:(g + 1) * TOK], sgB2[pb][:, :]),
                        waits=[("V", v_r)], inc="QA", inc_n=16))
                else:
                    w_list = [("T", t_mark), ("V", pr.c["V"]), ("LY", pr.c["LY"]),
                              ("T", pr.c["T"])]
                    sc_v = pr.emit("vector", lambda e, fg=fg, fsub=fsub, g=g, pb=pb: e.tensor_scalar(
                        sgA[pb][:, :], get_ps(fg, fsub)[:, :],
                        bv_s[:, g:g + 1], None, ALU.add), waits=w_list, inc="V")
                    evacs[2 * fg + fsub] = sc_v
                    v_r = None
                    for tb in range(8):
                        w2 = [("V", sc_v), ("V", pr.c["V"])]
                        t_tp = pr.emit("tensor", lambda e, pb=pb, tb=tb: e.transpose(
                            tp_ps[tb % 2][:, :], sgA[pb][:, tb * P:(tb + 1) * P],
                            i128b_s[:, :]), waits=w2, inc="T")
                        v_r = pr.emit("vector", lambda e, pb=pb, tb=tb: e.tensor_copy(
                            sgB2[pb][:, tb * P:(tb + 1) * P], tp_ps[tb % 2][:, :]),
                            waits=[("T", t_tp)], inc="V")
                    cc_marks.append(pr.emit("sync", lambda e, g=g, pb=pb: e.dma_start(
                        cc_in.ap()[:, G * TOK + g * TOK:G * TOK + (g + 1) * TOK],
                        sgB2[pb][:, :]),
                        waits=[("V", v_r)], inc="QB", inc_n=16))

        evacsK = {}

        def k_first_waits(fg):
            if fg >= 2:
                return [("V", evacsK[2 * (fg - 2) + 1])]
            if fg == 0:
                return [("V", pr.c["V"])]
            return []

        stream_gemm(ws_k, lambda fg, fsub: kv4[(2 * fg + fsub) % 4],
                    None, k_first_waits,
                    True, lambda fg, t: kv_evac_rope(0, fg, t, lambda a, b: kv4[(2 * a + b) % 4], evacsK))
        ws_v = make_ws(wv_t, 4)
        for u, ku in ((0, 6), (1, 7), (2, 5)):
            ws_load(ws_v, u, gate=ws_k["t_unit"][ku])


        scope_k23.close()
        kv2b = ps(scope_p1b, "kv2b", [P, TOK])
        tp_ps = [ps(scope_p1b, f"tp{i}", [P, P], BF16) for i in range(2)]
        vt3 = [kv01[0], kv01[1], kv2b]
        evacsV = {}

        def v_first_waits(fg):
            if fg == 0:
                return [("V", evacsK[5])]
            if fg == 1:
                return [("V", evacsV[0])]
            return [("V", evacsV[2 * fg - 2])]

        stream_gemm(ws_v, lambda fg, fsub: vt3[(2 * fg + fsub) % 3],
                    None, v_first_waits,
                    True, lambda fg, t: kv_evac_rope(1, fg, t, lambda a, b: vt3[(2 * a + b) % 3], evacsV))

        pr.wait("gpsimd", "QA", pr.c["QA"])
        pr.wait("gpsimd", "QB", pr.c["QB"])
        cc_m = pr.emit("gpsimd", lambda e: e.collective_compute(
            "AllGather", ALU.bypass,
            replica_groups=[[0, 1], [2, 3], [4, 5], [6, 7]],
            ins=[cc_in.ap()], outs=[cc_out.ap()]), inc="CC")
        pr.wait("sync", "CC", cc_m)
        pr.wait("scalar", "CC", cc_m)
        for q in range(2):
            eng = "sync" if q == 0 else "scalar"
            sq = "LY" if q == 0 else "SQ"
            pr.emit(eng, lambda e, q=q: e.dma_start(
                kT[:, :].rearrange("p (g c two i) -> p g c two i", g=G, two=2, i=P)[:, :, :, q, :],
                cc_out.ap()[q * P:(q + 1) * P, 0:G * TOK].rearrange(
                    "p (g j i) -> p g j i", g=G, i=P)), inc=sq, inc_n=16)
            pr.emit(eng, lambda e, q=q: e.dma_start(
                vT[:, :].rearrange("p (g c two i) -> p g c two i", g=G, two=2, i=P)[:, :, :, q, :],
                cc_out.ap()[q * P:(q + 1) * P, G * TOK:2 * G * TOK].rearrange(
                    "p (g j i) -> p g j i", g=G, i=P)), inc=sq, inc_n=16)
        if tap_kt is not None:
            pr.emit("sync", lambda e: e.dma_start(tap_kt, kT[:, :]), inc="LY", inc_n=16)
        if tap_vt is not None:
            pr.emit("sync", lambda e: e.dma_start(tap_vt, vT[:, :]), inc="LY", inc_n=16)
        pr.flush("p1b_kv")
        pr.join_all()
        scope_p1b.close()
        scope_sg.close()
        scope_1ab.close()

        scope_p2 = ExitStack()
        PT = sb(scope_p2, "PT", [P, PT_W], BF16)
        qh = [sb(scope_p2, f"qh{i}", [P, TOK], BF16) for i in range(2)]
        rZb = sb(scope_p2, "rZb", [P, TOK], F32)
        cstg = sb(scope_p2, "cstg", [P, TOK], F32)
        mask_s = sb(scope_p2, "mask_s", [P, NKC * P], BF16)
        stp = [ps(scope_p2, f"stp{i}", [P, TOK]) for i in range(2)]
        ctx_ps = ps(scope_p2, "ctx_ps", [P, TOK])
        z_ps = ps(scope_p2, "z_ps", [P, TOK])

        mask_mark = pr.emit("sync", lambda e: e.dma_start(
            mask_s[:, :].rearrange("p (k c) -> p k c", k=NKC),
            mk_h.rearrange("k p c -> p k c")), inc="LY", inc_n=16)
        qh_ld = {0: pr.emit("sync", lambda e: e.dma_start(qh[0][:, :], qbuf[0:P, :]),
                            inc="LY", inc_n=16)}

        t_p1 = None
        for m in range(8):
            kc = 2 * m
            lhs = kT[:, kc * P:(kc + 1) * P].rearrange("p (a s) -> p a s", s=4)[:, :, 0]
            t_p1 = pr.emit("tensor", lambda e, m=m, lhs=lhs: e.matmul(
                stp[0][0:32, m * 64:(m + 1) * 64], lhs, qh[0][:, 0:64],
                start=(m == 0), stop=(m == 7)),
                waits=[("LY", qh_ld[0])] if m == 0 else [],
                inc="T" if m == 7 else None)
        v_m1 = pr.emit("vector", lambda e: e.reduce_max(
            mx1[:, :], stp[0][0:32, 0:512], axis=AX.X), waits=[("T", t_p1)], inc="V")
        t_m1 = pr.emit("tensor", lambda e: e.matmul(
            stp[0][0:1, 512:544], mx1[:, :], i128f_s[0:32, 0:32], start=True, stop=True),
            waits=[("V", v_m1)], inc="T")
        v_m2 = pr.emit("vector", lambda e: e.reduce_max(
            mxs[:, :], stp[0][0:1, 512:544], axis=AX.X, negate=True),
            waits=[("T", t_m1)], inc="V")
        t_m2 = pr.emit("tensor", lambda e: e.matmul(
            stp[0][:, 576:577], ones1f[:, :], mxs[:, :], start=True, stop=True),
            waits=[("V", v_m2)], inc="T")
        v_m3 = pr.emit("vector", lambda e: e.tensor_copy(mneg[:, :], stp[0][:, 576:577]),
                       waits=[("T", t_m2)], inc="V")

        t_ctx = {}
        t_pv = {}
        v_div = {}
        v_rcp = {}
        prev_exp = {14: v_m3, 15: v_m3}
        for h in range(H):
            g = h // 4
            if h + 1 < H:
                w_list = [("T", t_ctx[h - 1])] if h >= 1 else []
                qh_ld[h + 1] = pr.emit("sync", lambda e, h=h: e.dma_start(
                    qh[(h + 1) % 2][:, :], qbuf[(h + 1) * P:(h + 2) * P, :]),
                    waits=w_list, inc="LY", inc_n=16)
            sc_exp = {}
            pl_msk = {}
            t_st = {}

            def emit_ST(kc, h=h, g=g, sc_exp=sc_exp, t_st=t_st):
                w = W_KC[kc]
                q0 = Q0[kc]
                buf = stp[kc % 2]
                first = True
                for (c0, c1) in _pieces(0, w):
                    w_list = []
                    if first:
                        first = False
                        if kc >= 2:
                            w_list.append(("Sc", sc_exp[kc - 2]))
                        elif h > 0:
                            w_list.append(("Sc", prev_exp[14 + kc]))
                        else:
                            w_list.append(("V", v_m3))
                        if kc == 0:
                            w_list.append(("LY", qh_ld[h]))
                    last = (c1 == w)
                    r = pr.emit("tensor", lambda e, g=g, kc=kc, c0=c0, c1=c1, q0=q0, h=h: e.matmul(
                        buf[:, c0:c1],
                        kT[:, g * S + kc * P:g * S + (kc + 1) * P],
                        qh[h % 2][:, q0 + c0: q0 + c1],
                        start=True, stop=True),
                        waits=w_list, inc="T" if last else None)
                    if last:
                        t_st[kc] = r

            def emit_PV(kc, h=h, g=g, pl_msk=pl_msk):
                q0k = Q0[kc]
                for (c0, c1) in _pieces(q0k, TOK):
                    stop = (kc == 7) if c0 < 512 else (kc == NKC - 1)
                    w_list = [("Pl", pl_msk[kc])]
                    if kc == 0 and h >= 1 and c0 == q0k:
                        w_list.append(("V", v_div[h - 1]))
                    last_pv = (kc == NKC - 1 and c1 == TOK)
                    r = pr.emit("tensor", lambda e, g=g, kc=kc, c0=c0, c1=c1, q0k=q0k, stop=stop: e.matmul(
                        ctx_ps[:, c0:c1],
                        vT[:, g * S + kc * P:g * S + (kc + 1) * P],
                        PT[:, OFF_KC[kc] + (c0 - q0k): OFF_KC[kc] + (c1 - q0k)],
                        start=(kc == 0), stop=stop),
                        waits=w_list, inc="T" if last_pv else None)
                    if last_pv:
                        t_pv[h] = r

            def emit_exp(kc, h=h, sc_exp=sc_exp, t_st=t_st):
                w = W_KC[kc]
                w_list = [("T", t_st[kc])]
                if h == 0 and kc <= 1:
                    w_list.append(("V", v_m3))
                sc_exp[kc] = pr.emit("scalar", lambda e, w=w, kc=kc: e.activation(
                    PT[:, OFF_KC[kc]:OFF_KC[kc] + w], stp[kc % 2][:, 0:w],
                    AF.Exp, bias=mneg[:, :], scale=1.0), waits=w_list, inc="Sc")

            def emit_msk(kc, h=h, sc_exp=sc_exp, pl_msk=pl_msk):
                wm_ = [("Sc", sc_exp[kc])]
                if h == 0 and kc == 0:
                    wm_.append(("LY", mask_mark))
                pl_msk[kc] = pr.emit("gpsimd", lambda e, kc=kc: e.tensor_tensor(
                    PT[:, OFF_KC[kc]:OFF_KC[kc] + P],
                    PT[:, OFF_KC[kc]:OFF_KC[kc] + P],
                    mask_s[:, kc * P:(kc + 1) * P], ALU.mult), waits=wm_, inc="Pl")

            for kc in range(NKC):
                emit_ST(kc)
                emit_exp(kc)
                emit_msk(kc)
                if kc >= 2:
                    emit_PV(kc - 2)
            emit_PV(NKC - 2)
            emit_PV(NKC - 1)
            prev_exp = {14: sc_exp[14], 15: sc_exp[15]}
            first = True
            for kc in range(NKC):
                q0k = Q0[kc]
                for (c0, c1) in _pieces(q0k, TOK):
                    w_list = []
                    if first:
                        if h >= 1:
                            w_list.append(("V", v_rcp[h - 1]))
                        first = False
                    last = (kc == NKC - 1 and c1 == TOK)
                    r = pr.emit("tensor", lambda e, kc=kc, c0=c0, c1=c1, q0k=q0k: e.matmul(
                        z_ps[:, c0:c1], ones128b[:, :],
                        PT[:, OFF_KC[kc] + (c0 - q0k): OFF_KC[kc] + (c1 - q0k)],
                        start=(kc == 0), stop=(kc == NKC - 1)),
                        waits=w_list, inc="T" if last else None)
                    if last:
                        t_ctx[h] = r
            v_div[h] = pr.emit("vector", lambda e: e.tensor_copy(cstg[:, :], ctx_ps[:, :]),
                               waits=[("T", t_pv[h])], inc="V")
            v_rcp[h] = pr.emit("vector", lambda e: e.reciprocal(rZb[:, :], z_ps[:, :]),
                               waits=[("T", t_ctx[h])], inc="V")
            pr.emit("vector", lambda e, h=h: e.tensor_tensor(
                big0[:, h * TOK:(h + 1) * TOK], cstg[:, :], rZb[:, :], ALU.mult), inc="V")
        if tap_ctx is not None:
            pr.emit("sync", lambda e: e.dma_start(tap_ctx, big0[:, :]),
                    waits=[("V", pr.c["V"])], inc="LY", inc_n=16)
        pr.flush("p2_attn")
        pr.join_all()
        scope_p2.close()
        scope_kv.close()

        scope_p37 = ExitStack()
        WT["bufs"] = [sb(scope_p37, f"wtb{i}", [P, 4096], BF16) for i in range(4)]
        z1s = [sb(scope_p37, f"z1s{i}", [P, TOK], F32) for i in range(3)]
        xres = [sb(scope_p37, f"xres{i}", [P, TOK], F32) for i in range(3)]
        zbf = [sb(scope_p37, f"zbf{i}", [P, TOK], BF16) for i in range(2)]
        zsq = [sb(scope_p37, f"zsq{i}", [P, TOK], BF16) for i in range(2)]
        rb_s = sb(scope_p37, "rb_s", [P, TOK], F32)
        mrs_s = sb(scope_p37, "mrs_s", [P, TOK], F32)
        s_mean = sb(scope_p37, "s_mean", [1, TOK], F32)
        s_tb = sb(scope_p37, "s_tb", [1, TOK], F32)
        s_tc = sb(scope_p37, "s_tc", [1, TOK], F32)

        def gemm_ln_phase(ws, bias_sb, outbuf, resload, deltas, wp, st_ps):
            res_mark = {0: resload(0, []), 1: resload(1, [])}
            v_stt = {}
            sc_sq = {}
            pl_cp = {}
            st_mark = {}
            st_t = {}

            def first_waits(fg):
                if fg == 1:
                    return [("V", v_stt[0])]
                if fg >= 2:
                    return [("V", v_stt[2 * fg - 2])]
                return []

            def emit_stats(fc):
                pb = fc % 2
                for nh in range(2):
                    pr.emit("tensor", lambda e, pb=pb, nh=nh, fc=fc: e.matmul(
                        st_ps[0:1, nh * 512:(nh + 1) * 512], onescb[:, :],
                        zbf[pb][:, nh * 512:(nh + 1) * 512],
                        start=(fc == 0), stop=False),
                        waits=[("Sc", pl_cp[fc])] if nh == 0 else [], inc=None)
                    last = (nh == 1)
                    r = pr.emit("tensor", lambda e, pb=pb, nh=nh, fc=fc: e.matmul(
                        st_ps[0:2, nh * 512:(nh + 1) * 512], zo2[:, :],
                        zsq[pb][:, nh * 512:(nh + 1) * 512],
                        start=False, stop=(fc == 31)),
                        waits=[("Sc", sc_sq[fc])] if nh == 0 else [],
                        inc="T" if last else None)
                    if last:
                        st_t[fc] = r

            def fg_done(fg, t_mark):
                if deltas is not None:
                    t_mark = deltas(fg)
                if fg >= 1:
                    emit_stats(2 * (fg - 1))
                    emit_stats(2 * (fg - 1) + 1)
                for fsub in range(2):
                    fc = 2 * fg + fsub
                    pb = fc % 2
                    w_list = [("T", t_mark), res_mark[fc]]
                    if fc >= 2:
                        w_list += [("Sc", max(sc_sq[fc - 2], pl_cp[fc - 2])),
                                   st_mark[fc - 2]]
                    pr.emit("vector", lambda e, fsub=fsub, fc=fc, pb=pb: e.tensor_scalar(
                        z1s[pb][:, :], wp[fc % 3][:, :], bias_sb[:, fc:fc + 1], None,
                        ALU.add), waits=w_list)
                    v_stt[fc] = pr.emit("vector", lambda e, pb=pb: e.tensor_tensor(
                        z1s[pb][:, :], z1s[pb][:, :], xres[pb][:, :], ALU.add), inc="V")
                    w_sq = [("V", v_stt[fc])]
                    if fc >= 2:
                        w_sq.append(("T", st_t[fc - 2]))
                    sc_sq[fc] = pr.emit("scalar", lambda e, pb=pb: e.activation(
                        zsq[pb][:, :], z1s[pb][:, :], AF.Square),
                        waits=w_sq, inc="Sc")
                    w_cp = [("V", v_stt[fc])]
                    if fc >= 2:
                        w_cp.append(("T", st_t[fc - 2]))
                    pl_cp[fc] = pr.emit("scalar", lambda e, pb=pb: e.copy(
                        zbf[pb][:, :], z1s[pb][:, :]), waits=w_cp, inc="Sc")
                    ms = f"M{fc % 2}"
                    st_mark[fc] = (ms, pr.emit("scalar", lambda e, fc=fc, pb=pb, outbuf=outbuf: e.dma_start(
                        outbuf[fc * P:(fc + 1) * P, :], z1s[pb][:, :]),
                        waits=[("V", v_stt[fc])], inc=ms, inc_n=16))
                for fc2 in (2 * fg + 2, 2 * fg + 3):
                    if fc2 < 32:
                        res_mark[fc2] = resload(fc2, [("V", v_stt[fc2 - 2])])

            stream_gemm(ws, lambda fg, fsub: wp[(2 * fg + fsub) % 3], None, first_waits,
                        deltas is None, fg_done)
            emit_stats(30)
            emit_stats(31)
            return st_mark

        def ln_finalize(st_ps, rbc_ps):
            sc_c = pr.emit("scalar", lambda e: e.copy(st2_sb[:, :], st_ps[:, :]),
                           inc="Sc")
            ly_r = pr.emit("sync", lambda e: e.dma_start(s2row[:, :], st2_sb[1:2, :]),
                           waits=[("Sc", sc_c)], inc="LY", inc_n=16)
            v0 = pr.emit("vector", lambda e: e.tensor_scalar(
                s_mean[:, :], st2_sb[0:1, :], 1.0 / DM, None, ALU.mult),
                waits=[("Sc", sc_c)], inc="V")
            v0 = pr.emit("vector", lambda e: e.tensor_scalar(
                s_tb[:, :], s2row[:, :], 1.0 / DM, None, ALU.mult),
                waits=[("LY", ly_r)], inc="V")
            v0 = pr.emit("vector", lambda e: e.tensor_tensor(
                s_tc[:, :], s_mean[:, :], s_mean[:, :], ALU.mult), inc="V")
            v0 = pr.emit("vector", lambda e: e.tensor_tensor(
                s_tb[:, :], s_tb[:, :], s_tc[:, :], ALU.subtract), inc="V")
            v0 = pr.emit("vector", lambda e: e.tensor_scalar(
                s_tb[:, :], s_tb[:, :], 1e-5, None, ALU.add), inc="V")
            sc0 = pr.emit("scalar", lambda e: e.activation(
                s_tc[:, :], s_tb[:, :], AF.Sqrt),
                waits=[("V", v0)], inc="Sc")
            v1 = pr.emit("vector", lambda e: e.reciprocal(s_tb[:, :], s_tc[:, :]),
                         waits=[("Sc", sc0)], inc="V")
            t0 = None
            for nh in range(2):
                t0 = pr.emit("tensor", lambda e, nh=nh: e.matmul(
                    rbc_ps[:, nh * 512:(nh + 1) * 512], ones1f[:, :],
                    s_tb[0:1, nh * 512:(nh + 1) * 512], start=True, stop=True),
                    waits=[("V", v1)], inc="T" if nh == 1 else None)
            v2 = pr.emit("vector", lambda e: e.tensor_copy(rb_s[:, :], rbc_ps[:, :]),
                         waits=[("T", t0)], inc="V")
            v2 = pr.emit("vector", lambda e: e.tensor_tensor(
                s_tc[:, :], s_mean[:, :], s_tb[:, :], ALU.mult), inc="V")
            for nh in range(2):
                t0 = pr.emit("tensor", lambda e, nh=nh: e.matmul(
                    rbc_ps[:, nh * 512:(nh + 1) * 512], ones1f[:, :],
                    s_tc[0:1, nh * 512:(nh + 1) * 512], start=True, stop=True),
                    waits=[("V", v2)], inc="T" if nh == 1 else None)
            v2 = pr.emit("vector", lambda e: e.tensor_copy(mrs_s[:, :], rbc_ps[:, :]),
                         waits=[("T", t0)], inc="V")
            return v2

        def ln_normalize(inbuf, in_marks, to_big0, outbuf, v_fin):
            marks = {}
            rd = {}
            vv = {}
            pp = {}
            ptag = "Sc" if to_big0 else "V"
            for fc in range(3):
                ns = f"N{fc % 3}"
                rd[fc] = (ns, pr.emit("sync", lambda e, fc=fc, inbuf=inbuf: e.dma_start(
                    z1s[fc % 3][:, :], inbuf[fc * P:(fc + 1) * P, :]),
                    waits=[in_marks[fc]], inc=ns, inc_n=16))
            for fc in range(NCH):
                pb = fc % 3
                w_list = [rd[fc], ("V", v_fin)]
                if fc >= 3:
                    w_list.append((ptag, pp[fc - 3]))
                vv[fc] = pr.emit("vector", lambda e, pb=pb: e.tensor_tensor(
                    xres[pb][:, :], z1s[pb][:, :], rb_s[:, :], ALU.mult),
                    waits=w_list, inc="V")
                pp[fc] = pr.emit("vector", lambda e, pb=pb: e.tensor_tensor(
                    z1s[pb][:, :], xres[pb][:, :], mrs_s[:, :], ALU.subtract), inc="V")
                if to_big0:
                    pp[fc] = pr.emit("scalar", lambda e, pb=pb, fc=fc: e.copy(
                        big0[:, fc * TOK:(fc + 1) * TOK], z1s[pb][:, :]),
                        waits=[("V", pp[fc])], inc="Sc")
                ms = f"M{fc % 3}"
                marks[fc] = (ms, pr.emit("scalar", lambda e, fc=fc, pb=pb, outbuf=outbuf: e.dma_start(
                    outbuf[fc * P:(fc + 1) * P, :], z1s[pb][:, :]),
                    waits=[(ptag, pp[fc])], inc=ms, inc_n=16))
                if fc + 3 < NCH:
                    ns = f"N{fc % 3}"
                    rd[fc + 3] = (ns, pr.emit("sync", lambda e, fc=fc, pb=pb, inbuf=inbuf: e.dma_start(
                        z1s[pb][:, :], inbuf[(fc + 3) * P:(fc + 4) * P, :]),
                        waits=[in_marks[fc + 3], (ptag, pp[fc]), marks[fc]],
                        inc=ns, inc_n=16))
            return marks

        scope_s12a = ExitStack()
        sta_ps = ps(scope_s12a, "sta_ps", [2, TOK])
        scope_wp3 = ExitStack()
        wp3 = [ps(scope_wp3, f"wp3{i}", [P, TOK]) for i in range(3)]

        def p3_resload(fc, waits):
            rs = f"R{fc % 2}"
            return (rs, pr.emit("sync", lambda e, fc=fc: e.dma_start(
                xres[fc % 2][:, :], xsel[fc * P:(fc + 1) * P, :]),
                waits=waits, inc=rs, inc_n=16))

        ws_o = make_ws(wo_t, 16)
        z1_marks = gemm_ln_phase(ws_o, bo_s, z1buf, p3_resload, None, wp3, sta_ps)
        pr.flush("p3_wo")
        t_after_p3 = pr.c["T"]
        pr.join_all()
        scope_wp3.close()

        scope_p4 = ExitStack()
        rbc4 = ps(scope_p4, "rbc4", [P, TOK])
        v_fin1 = ln_finalize(sta_ps, rbc4)
        y1_marks = ln_normalize(z1buf, z1_marks, True, y1buf, v_fin1)
        ws_f = make_ws(kn_t, 16, init_gate=t_after_p3)
        for u in range(4):
            ws_load(ws_f, u)
        pr.flush("p4_ln1")
        pr.join_all()
        scope_p4.close()
        scope_s12a.close()

        scope_p5 = ExitStack()
        axp = ps(scope_p5, "axp", [P, TOK])
        t_ax = None
        for ch in range(NCH):
            w_list = [("LG", lg_asht)] if ch == 0 else []
            for nh in range(2):
                last = (ch == NCH - 1 and nh == 1)
                t_ax = pr.emit("tensor", lambda e, ch=ch, nh=nh: e.matmul(
                    axp[:, nh * 512:(nh + 1) * 512],
                    A_comb[:, ch * 128:(ch + 1) * 128],
                    big0[:, ch * TOK + nh * 512: ch * TOK + (nh + 1) * 512],
                    start=(ch == 0), stop=(ch == NCH - 1)),
                    waits=w_list if nh == 0 else [], inc="T" if last else None)
        sc_ax = pr.emit("scalar", lambda e: e.copy(Ax_sb[:, :], axp[:, :]),
                waits=[("T", t_ax)], inc="Sc")
        if tap_ax is not None:
            pr.emit("sync", lambda e: e.dma_start(tap_ax, Ax_sb[:, :]),
                    waits=[("Sc", sc_ax)], inc="LY", inc_n=16)
            pr.emit("sync", lambda e: e.dma_start(tap_acomb, A_comb[:, :]), inc="LY", inc_n=16)
            pr.emit("sync", lambda e: e.dma_start(tap_bstack, B_stack[:, :]), inc="LY", inc_n=16)
        pr.flush("p5_ax")
        pr.join_all()
        scope_p5.close()

        scope_p6 = ExitStack()
        stb_ps = ps(scope_p6, "stb_ps", [2, TOK])
        scope_wp6 = ExitStack()
        wp6 = [ps(scope_wp6, f"wp6{i}", [P, TOK]) for i in range(3)]

        def p6_resload(fc, waits):
            rs = f"R{fc % 2}"
            return (rs, pr.emit("sync", lambda e, fc=fc: e.dma_start(
                xres[fc % 2][:, :], y1buf[fc * P:(fc + 1) * P, :]),
                waits=list(waits) + [y1_marks[fc]], inc=rs, inc_n=16))

        def p6_deltas(fg):
            t_l = None
            first = True
            for fsub in range(2):
                fc = 2 * fg + fsub
                for half in range(2):
                    r0 = 0 if half == 0 else 64
                    w_list = [("LG", lg_bsht), ("LY", b_last_mark)] if (fg == 0 and first) else []
                    first = False
                    last = (fsub == 1 and half == 1)
                    t_l = pr.emit("tensor", lambda e, fc=fc, fsub=fsub, half=half, r0=r0: e.matmul(
                        wp6[fc % 3][:, half * 512:(half + 1) * 512],
                        B_stack[r0:r0 + 64, fc * P:(fc + 1) * P],
                        Ax_sb[r0:r0 + 64, half * 512:(half + 1) * 512],
                        start=False, stop=True),
                        waits=w_list, inc="T" if last else None)
            return t_l

        z2_marks = gemm_ln_phase(ws_f, bi_s, z2buf, p6_resload, p6_deltas, wp6, stb_ps)
        pr.flush("p6_ffn")
        pr.join_all()
        scope_wp6.close()

        rbc7 = ps(scope_p6, "rbc7", [P, TOK])
        v_fin2 = ln_finalize(stb_ps, rbc7)
        out_marks = ln_normalize(z2buf, z2_marks, False, outT, v_fin2)
        for ms in ("M0", "M1", "M2"):
            pr.wait("sync", ms, pr.c[ms])
        pr.flush("p7_out")
        scope_p6.close()
        scope_p37.close()
        scope_big.close()

    return nc



def _pretile(w):
    nout = w.shape[1]
    return np.ascontiguousarray(
        w.reshape(32, 128, nout // 256, 256).transpose(2, 1, 0, 3))


def _host_inputs(L0_attn_hidden_states, L0_attn_Wq, L0_attn_bq, L0_attn_Wk, L0_attn_bk,
                 L0_attn_Wv, L0_attn_bv, L0_attn_Wo, L0_attn_bo, L0_ffn_gate,
                 L0_ffn_kernel, L0_ffn_bias, L0_ffn_lora_A, L0_ffn_lora_B,
                 L0_ffn_lora_A_shared, L0_ffn_lora_B_shared):
    f32 = np.float32
    hid = np.asarray(L0_attn_hidden_states, f32)
    shared = {
        "wq_t": _pretile(np.asarray(L0_attn_Wq, f32)),
        "wk_t": _pretile(np.asarray(L0_attn_Wk, f32)),
        "wv_t": _pretile(np.asarray(L0_attn_Wv, f32)),
        "wo_t": _pretile(np.asarray(L0_attn_Wo, f32)),
        "kn_t": _pretile(np.asarray(L0_ffn_kernel, f32)),
        "bq_r": np.ascontiguousarray(np.asarray(L0_attn_bq, f32).reshape(32, 128).T),
        "bk_r": np.ascontiguousarray(np.asarray(L0_attn_bk, f32).reshape(8, 128).T),
        "bv_r": np.ascontiguousarray(np.asarray(L0_attn_bv, f32).reshape(8, 128).T),
        "bo_r": np.ascontiguousarray(np.asarray(L0_attn_bo, f32).reshape(32, 128).T),
        "bi_r": np.ascontiguousarray(np.asarray(L0_ffn_bias, f32).reshape(32, 128).T),
        "la2": np.ascontiguousarray(np.asarray(L0_ffn_lora_A, f32).reshape(512, DM)),
        "lbt": np.ascontiguousarray(
            np.asarray(L0_ffn_lora_B, f32).transpose(0, 2, 1)).reshape(512, DM),
        "asht": np.ascontiguousarray(np.asarray(L0_ffn_lora_A_shared, f32).T),
        "bsht": np.ascontiguousarray(np.asarray(L0_ffn_lora_B_shared, f32).T),
    }
    esel = np.zeros((16, 512), f32)
    for a in range(4):
        for pp in range(128):
            esel[4 * a + pp // 32, a * 128 + pp] = 1.0
    i32s = np.zeros((128, 32), f32)
    for pp in range(128):
        i32s[pp, pp % 32] = 1.0
    shared["esel_h"] = esel
    shared["i32s_h"] = i32s.astype(BF)
    shared["i128b_h"] = np.eye(128).astype(BF)
    shared["i128f_h"] = np.eye(128, dtype=f32)

    inv_freq = (1.0 / (10000.0 ** (np.arange(64, dtype=np.float64) / 64.0))).astype(f32)

    def dup_trig(ang, scale):
        c = np.cos(ang) * scale
        s = np.sin(ang) * scale
        C = np.concatenate([c, c], axis=0)
        Ss = np.concatenate([-s, s], axis=0)
        return C.astype(BF), Ss.astype(BF)



    tri = (np.arange(P)[:, None] <= np.arange(P)[None, :]).astype(f32)

    gate = np.asarray(L0_ffn_gate, f32)
    gT = np.ascontiguousarray(gate.T)

    in_maps = []
    for core in range(8):
        b, p = core // 2, core % 2
        xTb = np.ascontiguousarray(hid[b].T)
        own_cols = np.concatenate([np.arange((2 * j + p) * 128, (2 * j + p) * 128 + 128)
                                   for j in range(8)])
        xsel_a = np.ascontiguousarray(xTb[:, own_cols])
        pos_q = own_cols.astype(f32)
        ang_q = inv_freq[:, None] * pos_q[None, :]
        scale = 1.0 / math.sqrt(D)
        mk = np.zeros((NKC, P, P), f32)
        for kc in range(NKC):
            if p == 0:
                mk[kc] = tri if kc % 2 == 0 else 0.0
            else:
                mk[kc] = 1.0 if kc % 2 == 0 else tri
        m = dict(shared)
        m["xsel"] = xsel_a
        m["cq_h"], m["sq_h"] = dup_trig(ang_q, scale)
        m["ckq_h"], m["skq_h"] = dup_trig(ang_q, 1.0)
        m["mk_h"] = mk.astype(BF)
        m["g2"] = np.ascontiguousarray(gT[:, [2 * b, 2 * b + 1]])
        in_maps.append(m)
    return in_maps


def kernel(**inputs):
    in_maps = _host_inputs(**inputs)
    nc = build_nc()
    res = run_bass_kernel_spmd(nc, in_maps, core_ids=list(range(8)))
    out = np.empty((B, S, DM), np.float32)
    for core in range(8):
        b, p = core // 2, core % 2
        y = res.results[core]["outT"]
        yt = np.ascontiguousarray(y.T).reshape(8, 128, DM)
        out[b].reshape(16, 128, DM)[p::2] = yt
    return out


# version 24
# speedup vs baseline: 1.0039x; 1.0039x over previous
import math
from contextlib import ExitStack

import numpy as np
import ml_dtypes

import concourse.bass as bass
import concourse.mybir as mybir
from concourse.bass_utils import run_bass_kernel_spmd

F32 = mybir.dt.float32
BF16 = mybir.dt.bfloat16
AF = mybir.ActivationFunctionType
ALU = mybir.AluOpType
AX = mybir.AxisListType
BF = ml_dtypes.bfloat16

P = 128
DM = 4096
NCH = DM // P
S = 2048
B = 4
H = 32
G = 8
D = 128
TOK = 1024
NKC = S // P

ENGINES = ("tensor", "scalar", "vector", "gpsimd", "sync")

Q0 = [128 * (kc // 2) for kc in range(NKC)]
W_KC = [TOK - q0 for q0 in Q0]
OFF_KC = [int(x) for x in np.concatenate([[0], np.cumsum(W_KC)])]
PT_W = OFF_KC[-1]


def _pieces(c0, c1, align=512):
    out = []
    c = c0
    while c < c1:
        nxt = min(c1, (c // align + 1) * align)
        out.append((c, nxt))
        c = nxt
    return out


class Prog:

    def __init__(self, nc, sems):
        self.nc = nc
        self.sems = sems
        self.q = {e: [] for e in ENGINES}
        self.c = {k: 0 for k in sems}
        self.serialize = False

    def emit(self, eng, fn, waits=(), inc=None, inc_n=1):
        if self.serialize:
            waits = tuple(waits) + tuple((k, v) for k, v in self.c.items() if v > 0)
        waits = tuple(waits)
        for s, v in waits:
            assert v <= self.c[s], f"wait on {s}>={v} but only {self.c[s]} scheduled"
        sems = self.sems

        def run(e, fn=fn, waits=waits, inc=inc, inc_n=inc_n):
            for s, v in waits:
                e.wait_ge(sems[s], v)
            r = fn(e)
            if inc is not None:
                r.then_inc(sems[inc], inc_n)
        self.q[eng].append(run)
        if inc is not None:
            self.c[inc] += inc_n
        return self.c[inc] if inc is not None else None

    def join_all(self):
        snap = [(k, v) for k, v in self.c.items() if v > 0]
        for e in ENGINES:
            if e == "gpsimd":
                continue
            for k, v in snap:
                self.wait(e, k, v)

    def wait(self, eng, sname, val):
        assert val <= self.c[sname], f"wait on {sname}>={val} but only {self.c[sname]}"
        sems = self.sems
        self.q[eng].append(lambda e, s=sname, v=val: e.wait_ge(sems[s], v))

    def flush(self, name):
        if name != "p7_out":
            return
        with self.nc.named_scope(name), self.nc.Block(name) as block:
            for e in ENGINES:
                ops = self.q[e]
                if not ops:
                    continue
                getattr(block, e)(lambda engine, ops=tuple(ops): [op(engine) for op in ops])
        self.q = {e: [] for e in ENGINES}


def build_nc(taps=()):
    nc = bass.Bass("TRN2", target_bir_lowering=False)

    def dram_in(name, shape, dt):
        return nc.dram_tensor(name, shape, dt, kind="ExternalInput").ap()

    def dram_scratch(name, shape, dt):
        if name in taps:
            return nc.dram_tensor(name, shape, dt, kind="ExternalOutput").ap()
        return nc.dram_tensor(name, shape, dt).ap()

    xsel = dram_in("xsel", [DM, TOK], F32)
    wq_t = dram_in("wq_t", [16, P, 32, 256], F32)
    wk_t = dram_in("wk_t", [4, P, 32, 256], F32)
    wv_t = dram_in("wv_t", [4, P, 32, 256], F32)
    wo_t = dram_in("wo_t", [16, P, 32, 256], F32)
    kn_t = dram_in("kn_t", [16, P, 32, 256], F32)
    bq_r = dram_in("bq_r", [P, 32], F32)
    bk_r = dram_in("bk_r", [P, 8], F32)
    bv_r = dram_in("bv_r", [P, 8], F32)
    bo_r = dram_in("bo_r", [P, 32], F32)
    bi_r = dram_in("bi_r", [P, 32], F32)
    g2 = dram_in("g2", [16, 2], F32)
    la2 = dram_in("la2", [512, DM], F32)
    lbt = dram_in("lbt", [512, DM], F32)
    asht = dram_in("asht", [DM, 32], F32)
    bsht = dram_in("bsht", [32, DM], F32)
    cq_h = dram_in("cq_h", [P, TOK], BF16)
    sq_h = dram_in("sq_h", [P, TOK], BF16)
    ckq_h = dram_in("ckq_h", [P, TOK], BF16)
    skq_h = dram_in("skq_h", [P, TOK], BF16)
    mk_h = dram_in("mk_h", [NKC, P, P], BF16)
    esel_h = dram_in("esel_h", [16, 512], F32)
    i32s_h = dram_in("i32s_h", [P, 32], BF16)
    i128b_h = dram_in("i128b_h", [P, P], BF16)
    i128f_h = dram_in("i128f_h", [P, P], F32)

    qbuf = dram_scratch("qbuf", [DM, TOK], BF16)
    cc_in = nc.dram_tensor("cc_in", [P, 2 * G * TOK], BF16)
    cc_out = nc.dram_tensor("cc_out", [2 * P, 2 * G * TOK], BF16)
    z1buf = dram_scratch("z1buf", [DM, TOK], F32)
    y1buf = dram_scratch("y1buf", [DM, TOK], F32)
    z2buf = dram_scratch("z2buf", [DM, TOK], F32)
    outT = nc.dram_tensor("outT", [DM, TOK], F32, kind="ExternalOutput").ap()

    tap_kt = nc.dram_tensor("tap_kt", [P, G * S], BF16, kind="ExternalOutput").ap() if "kt" in taps else None
    tap_vt = nc.dram_tensor("tap_vt", [P, G * S], BF16, kind="ExternalOutput").ap() if "vt" in taps else None
    tap_ctx = nc.dram_tensor("tap_ctx", [P, NCH * TOK], BF16, kind="ExternalOutput").ap() if "ctx" in taps else None
    tap_ax = nc.dram_tensor("tap_ax", [P, TOK], BF16, kind="ExternalOutput").ap() if "ax" in taps else None
    tap_acomb = nc.dram_tensor("tap_acomb", [P, NCH * 128], BF16, kind="ExternalOutput").ap() if "ax" in taps else None
    tap_bstack = nc.dram_tensor("tap_bstack", [P, DM], BF16, kind="ExternalOutput").ap() if "ax" in taps else None

    stack = ExitStack()
    with stack:
        def sb(st, name, shape, dt):
            return st.enter_context(nc.sbuf_tensor(name, shape, dt))

        def ps(st, name, shape, dt=F32):
            return st.enter_context(nc.psum_tensor(name, shape, dt))

        bq_s = sb(stack, "bq_s", [P, 32], F32)
        bk_s = sb(stack, "bk_s", [P, 8], F32)
        bv_s = sb(stack, "bv_s", [P, 8], F32)
        bo_s = sb(stack, "bo_s", [P, 32], F32)
        bi_s = sb(stack, "bi_s", [P, 32], F32)
        g2_s = sb(stack, "g2_s", [16, 2], F32)
        esel_s = sb(stack, "esel_s", [16, 512], F32)
        i32s_s = sb(stack, "i32s_s", [P, 32], BF16)
        i128b_s = sb(stack, "i128b_s", [P, P], BF16)
        i128f_s = sb(stack, "i128f_s", [P, P], F32)
        ones128b = sb(stack, "ones128b", [P, P], BF16)
        ones1f = sb(stack, "ones1f", [1, P], F32)
        onescb = sb(stack, "onescb", [P, 1], BF16)
        zo2 = sb(stack, "zo2", [P, 2], BF16)
        st2_sb = sb(stack, "st2_sb", [2, TOK], F32)
        s2row = sb(stack, "s2row", [1, TOK], F32)
        A_comb = sb(stack, "A_comb", [P, NCH * 128], BF16)
        B_stack = sb(stack, "B_stack", [P, DM], BF16)
        Ax_sb = sb(stack, "Ax_sb", [P, TOK], BF16)
        gdelta_s = sb(stack, "gdelta_s", [P, 8 * 32], BF16)
        gr_sb = sb(stack, "gr_sb", [P, 1], F32)
        mneg = sb(stack, "mneg", [P, 1], F32)
        mx1 = sb(stack, "mx1", [32, 1], F32)
        mxs = sb(stack, "mxs", [1, 1], F32)

        sm = {k: stack.enter_context(nc.semaphore(k))
              for k in ("T", "Sc", "V", "Pl", "LG", "LY", "SQ", "CC",
                        "N0", "N1", "N2", "R0", "R1", "W0", "W1", "W2", "W3",
                        "X0", "X1", "X2", "X3", "M0", "M1", "M2", "QA", "QB")}
        pr = Prog(nc, sm)

        WT = {}
        import os
        _ser = os.environ.get("KSER", "")
        

        def make_ws(wdram, nfg, init_gate=None):
            return {"wdram": wdram, "nfg": nfg, "nbuf": len(WT["bufs"]),
                    "entry_gate": pr.c["T"] if init_gate is None else init_gate,
                    "loads": {}, "t_unit": {}}

        def ws_load(ws, u, gate=None):
            if u in ws["loads"]:
                return ws["loads"][u]
            nbuf = ws["nbuf"]
            if gate is not None:
                pr.wait("gpsimd", "T", gate)
            elif u >= nbuf:
                pr.wait("gpsimd", "T", ws["t_unit"][u - nbuf])
            else:
                pr.wait("gpsimd", "T", ws["entry_gate"])
            fg, hf = u // 2, u % 2
            wsem = f"W{u % nbuf}"
            mrk = pr.emit("gpsimd", lambda e, hf=hf, fg=fg, u=u, nbuf=nbuf, wdram=ws["wdram"]: e.dma_start(
                WT["bufs"][u % nbuf][:, :].rearrange("p (d c) -> p d c", c=256),
                wdram[fg, :, hf * 16:(hf + 1) * 16, :]),
                inc=wsem, inc_n=16)
            ws["loads"][u] = (wsem, mrk)
            return ws["loads"][u]

        def stream_gemm(ws, get_psum, rhs_marks, first_fg_waits,
                        plain_stop, on_fg_done, mid_cb=None):
            t_unit = ws["t_unit"]
            t_fg = {}
            nfg = ws["nfg"]
            nbuf = ws["nbuf"]
            for fg in range(nfg):
                for hf in range(2):
                    u = 2 * fg + hf
                    wsem, mrk = ws_load(ws, u)
                    if u == 1 and mid_cb is not None:
                        mid_cb()
                    for gi in range(2):
                        if gi == 0:
                            pr.wait("tensor", wsem, mrk)
                        if rhs_marks is not None:
                            rs, rv = rhs_marks[hf * 16 + gi * 8 + 7]
                            pr.wait("tensor", rs, rv)
                        if hf == 0 and gi == 0:
                            for s_, v_ in first_fg_waits(fg):
                                pr.wait("tensor", s_, v_)
                        for dd in range(gi * 8, gi * 8 + 8):
                            d = hf * 16 + dd
                            for fsub in range(2):
                                for nh in range(2):
                                    last_u = (dd == 15 and fsub == 1 and nh == 1)
                                    last_fg = (d == NCH - 1 and fsub == 1 and nh == 1)
                                    r = pr.emit("tensor", lambda e, fg=fg, u=u, dd=dd, d=d, fsub=fsub, nh=nh, nbuf=nbuf: e.matmul(
                                        get_psum(fg, fsub)[:, nh * 512:(nh + 1) * 512],
                                        WT["bufs"][u % nbuf][:, dd * 256 + fsub * 128: dd * 256 + fsub * 128 + 128],
                                        big0[:, d * TOK + nh * 512: d * TOK + (nh + 1) * 512],
                                        start=(d == 0),
                                        stop=(d == NCH - 1 and plain_stop)),
                                        inc="T" if last_u else None)
                                    if last_u:
                                        t_unit[u] = r
                                    if last_fg:
                                        t_fg[fg] = r
                on_fg_done(fg, t_fg[fg])
            return t_fg

        pr.serialize = "b0" in _ser
        ly_consts = None
        for dst, src in ((bq_s, bq_r), (bk_s, bk_r), (bv_s, bv_r), (bo_s, bo_r),
                         (bi_s, bi_r), (g2_s, g2), (esel_s, esel_h), (i32s_s, i32s_h),
                         (i128b_s, i128b_h), (i128f_s, i128f_h)):
            ly_consts = pr.emit("sync", lambda e, d=dst, s=src: e.dma_start(d[:, :], s),
                                inc="LY", inc_n=16)
        pr.emit("gpsimd", lambda e: e.memset(ones128b[:, :], 1.0))
        pr.emit("gpsimd", lambda e: e.memset(ones1f[:, :], 1.0))
        pr.emit("gpsimd", lambda e: e.memset(onescb[:, :], 1.0))
        pr.emit("gpsimd", lambda e: e.memset(zo2[:, 0:1], 0.0))
        pr.emit("gpsimd", lambda e: e.memset(zo2[:, 1:2], 1.0), inc="Pl")
        pr.emit("gpsimd", lambda e: e.dma_start(
            A_comb[:, :].rearrange("p (ch c) -> p ch c", c=128)[:, :, 32:64],
            asht.rearrange("(ch p) r -> p ch r", p=P)), inc="LG", inc_n=16)
        lg_asht = pr.emit("gpsimd", lambda e: e.dma_start(
            A_comb[:, :].rearrange("p (ch c) -> p ch c", c=128)[:, :, 96:128],
            asht.rearrange("(ch p) r -> p ch r", p=P)), inc="LG", inc_n=16)
        pr.emit("gpsimd", lambda e: e.dma_start(B_stack[32:64, :], bsht),
                inc="LG", inc_n=16)
        lg_bsht = pr.emit("gpsimd", lambda e: e.dma_start(B_stack[96:128, :], bsht),
                          inc="LG", inc_n=16)

        scope_big = ExitStack()
        big0 = sb(scope_big, "big0", [P, NCH * TOK], BF16)
        scope_kv = ExitStack()
        kT = sb(scope_kv, "kT", [P, G * S], BF16)
        vT = sb(scope_kv, "vT", [P, G * S], BF16)
        scope_1ab = ExitStack()
        WT["bufs"] = [sb(scope_1ab, f"wta{i}", [P, 4096], BF16) for i in range(3)]
        sgA = [sb(scope_1ab, f"sgA{i}", [P, TOK], BF16) for i in range(2)]

        scope_b0 = ExitStack()
        lw = [sb(scope_b0, f"lw{i}", [P, 2048], BF16) for i in range(4)]
        ae_sb = sb(scope_b0, "ae_sb", [32, 2048], BF16)
        gr_ps = ps(scope_b0, "gr_ps", [P, 1])
        ae_ps = ps(scope_b0, "ae_ps", [32, 2048])
        aet_ps = ps(scope_b0, "aet_ps", [P, 32], BF16)

        pr.serialize = True
        v_prev = 0
        for vb in range(2):
            for a in range(4):
                t_gr = pr.emit("tensor", lambda e, a=a, vb=vb: e.matmul(
                    gr_ps[:, :], esel_s[:, a * P:(a + 1) * P], g2_s[:, vb:vb + 1],
                    start=True, stop=True),
                    waits=[("LY", ly_consts), ("V", v_prev)], inc="T")
                v_gr = pr.emit("vector", lambda e: e.tensor_copy(gr_sb[:, :], gr_ps[:, :]),
                               waits=[("T", t_gr)], inc="V")
                v_prev = pr.emit("vector", lambda e, vb=vb, a=a: e.tensor_scalar(
                    gdelta_s[:, (vb * 4 + a) * 32:(vb * 4 + a + 1) * 32],
                    i32s_s[:, :], gr_sb[:, :], None, ALU.mult), inc="V")
        v_gd_done = v_prev
        pr.serialize = False

        b_last_mark = None
        for branch, srcdram in (("A", la2), ("B", lbt)):
            for half in range(2):
                lg_marks = []
                pr.wait("gpsimd", "T", pr.c["T"])
                for rc in range(4):
                    lg = pr.emit("gpsimd", lambda e, rc=rc, half=half, sd=srcdram: e.dma_start(
                        lw[rc][:, :], sd[rc * P:(rc + 1) * P, half * 2048:(half + 1) * 2048]),
                        inc=f"X{rc}", inc_n=16)
                    lg_marks.append((f"X{rc}", lg))
                for vb in range(2):
                    t_last = None
                    for rc in range(4):
                        for ccp in range(4):
                            w_list = [lg_marks[rc]]
                            if rc == 0:
                                w_list += [("V", v_gd_done), ("Sc", pr.c["Sc"])]
                            t_last = pr.emit("tensor", lambda e, vb=vb, rc=rc, ccp=ccp: e.matmul(
                                ae_ps[:, ccp * 512:(ccp + 1) * 512],
                                gdelta_s[:, (vb * 4 + rc) * 32:(vb * 4 + rc + 1) * 32],
                                lw[rc][:, ccp * 512:(ccp + 1) * 512],
                                start=(rc == 0), stop=(rc == 3)),
                                waits=w_list if ccp == 0 else [],
                                inc="T" if (rc == 3 and ccp == 3) else None)
                    if branch == "B":
                        r0 = 0 if vb == 0 else 64
                        w_c = [("T", t_last)]
                        if b_last_mark is not None:
                            w_c.append(("LY", b_last_mark))
                        sc_b = pr.emit("scalar", lambda e: e.copy(ae_sb[:, :], ae_ps[:, :]),
                                       waits=w_c, inc="Sc")
                        b_last_mark = pr.emit("sync", lambda e, r0=r0, half=half: e.dma_start(
                            B_stack[r0:r0 + 32, half * 2048:(half + 1) * 2048], ae_sb[:, :]),
                            waits=[("Sc", sc_b)], inc="LY", inc_n=16)
                    else:
                        sc_a = pr.emit("scalar", lambda e: e.copy(ae_sb[:, :], ae_ps[:, :]),
                                       waits=[("T", t_last)], inc="Sc")
                        voff = 0 if vb == 0 else 64
                        for chh in range(16):
                            ch = half * 16 + chh
                            t_tp = pr.emit("tensor", lambda e, chh=chh: e.transpose(
                                aet_ps[:, :], ae_sb[:, chh * P:(chh + 1) * P], i128b_s[0:32, 0:32]),
                                waits=[("Sc", sc_a), ("V", pr.c["V"])], inc="T")
                            pr.emit("vector", lambda e, ch=ch, voff=voff: e.tensor_copy(
                                A_comb[:, ch * 128 + voff: ch * 128 + voff + 32], aet_ps[:, :]),
                                waits=[("T", t_tp)], inc="V")
        pr.serialize = False

        lg_xfb = {}

        def _xsel_load(g):
            xs = f"X{g}"
            mk = pr.emit("gpsimd", lambda e, g=g: e.dma_start(
                big0[:, g * 8 * TOK:(g + 1) * 8 * TOK].rearrange(
                    "p (c n) -> p c n", c=8),
                xsel[g * 8 * P:(g + 1) * 8 * P, :].rearrange(
                    "(c p) n -> p c n", p=P)),
                inc=xs, inc_n=16)
            for d in range(8 * g, 8 * g + 8):
                lg_xfb[d] = (xs, mk)

        ws_q = make_ws(wq_t, 16)
        pr.wait("gpsimd", "T", pr.c["T"])
        _xsel_load(0)
        _xsel_load(1)
        ws_load(ws_q, 0)
        ws_load(ws_q, 1)
        _xsel_load(2)
        _xsel_load(3)
        ws_load(ws_q, 2)

        pr.flush("b0_prep")
        pr.join_all()
        scope_b0.close()
        scope_sg = ExitStack()
        swp = [sb(scope_sg, f"swp{i}", [P, TOK], BF16) for i in range(2)]
        tmpr = [sb(scope_sg, f"tmpr{i}", [P, TOK], BF16) for i in range(2)]

        scope_p1a = ExitStack()
        sgB = [sb(scope_p1a, f"sgB{i}", [P, TOK], BF16) for i in range(2)]
        cq_s = sb(scope_p1a, "cq_s", [P, TOK], BF16)
        sq_s = sb(scope_p1a, "sq_s", [P, TOK], BF16)
        qp = [ps(scope_p1a, f"qp{i}", [P, TOK]) for i in range(4)]

        ly_trig = None
        for dst, src in ((cq_s, cq_h), (sq_s, sq_h)):
            ly_trig = pr.emit("sync", lambda e, d=dst, s=src: e.dma_start(d[:, :], s),
                              inc="LY", inc_n=16)

        sc_evac = {}
        rope_done = {}
        sq_store = {}

        def p1a_first_waits(fg):
            if fg >= 2:
                return [("V", sc_evac[2 * (fg - 2) + 1])]
            return []

        def p1a_fg_done(fg, t_mark):
            for fsub in range(2):
                fc = 2 * fg + fsub
                pb = fc % 2
                w_list = [("T", t_mark)]
                if fc >= 2:
                    w_list.append(rope_done[fc - 2])
                sc_evac[fc] = pr.emit("vector", lambda e, fg=fg, fsub=fsub, fc=fc, pb=pb: e.tensor_scalar(
                    sgA[pb][:, :], qp[2 * (fg % 2) + fsub][:, :],
                    bq_s[:, fc:fc + 1], None, ALU.add), waits=w_list, inc="V")
                w_sw = [("V", sc_evac[fc])]
                if fc >= 2:
                    w_sw.append(rope_done[fc - 2])
                pr.emit("sync", lambda e, pb=pb: e.dma_start(
                    swp[pb][0:64, :], sgA[pb][64:128, :]), waits=w_sw, inc="LY", inc_n=16)
                sw_mark = pr.emit("sync", lambda e, pb=pb: e.dma_start(
                    swp[pb][64:128, :], sgA[pb][0:64, :]), inc="LY", inc_n=16)
                eng, esem = ("vector", "V")
                w_list = [("V", sc_evac[fc]), ("LY", sw_mark)]
                if fc < 2:
                    w_list.append(("LY", ly_trig))
                else:
                    w_list.append(sq_store[fc - 2])
                pr.emit(eng, lambda e, pb=pb: e.tensor_tensor(
                    tmpr[pb][:, :], swp[pb][:, :], sq_s[:, :], ALU.mult), waits=w_list)
                pr.emit(eng, lambda e, pb=pb: e.tensor_tensor(
                    sgB[pb][:, :], sgA[pb][:, :], cq_s[:, :], ALU.mult))
                rope_done[fc] = (esem, pr.emit(eng, lambda e, pb=pb: e.tensor_tensor(
                    sgB[pb][:, :], sgB[pb][:, :], tmpr[pb][:, :], ALU.add), inc=esem))
                qs = "QA" if fc % 2 == 0 else "QB"
                sq_store[fc] = (qs, pr.emit("sync", lambda e, fc=fc, pb=pb: e.dma_start(
                    qbuf[fc * P:(fc + 1) * P, :], sgB[pb][:, :]),
                    waits=[rope_done[fc]], inc=qs, inc_n=16))

        stream_gemm(ws_q, lambda fg, fsub: qp[2 * (fg % 2) + fsub],
                    lg_xfb, p1a_first_waits, True, p1a_fg_done)
        ws_k = make_ws(wk_t, 4)
        for u in range(3):
            ws_load(ws_k, u, gate=ws_q["t_unit"][29 + (u + 1) % 3])
        pr.flush("p1a_q")
        pr.join_all()
        scope_p1a.close()
        t_after_p1a = pr.c["T"]

        scope_p1b = ExitStack()
        ckq_s = sb(scope_p1b, "ckq_s", [P, TOK], BF16)
        skq_s = sb(scope_p1b, "skq_s", [P, TOK], BF16)
        sgB2 = [sb(scope_p1b, f"sgB2{i}", [P, TOK], BF16) for i in range(2)]
        kv01 = [ps(scope_p1b, f"kv{i}", [P, TOK]) for i in range(2)]
        scope_k23 = ExitStack()
        kv23 = [ps(scope_k23, f"kv{i+2}", [P, TOK]) for i in range(2)]
        kv4 = kv01 + kv23

        ly_trig = None
        for dst, src in ((ckq_s, ckq_h), (skq_s, skq_h)):
            ly_trig = pr.emit("sync", lambda e, d=dst, s=src: e.dma_start(d[:, :], s),
                              inc="LY", inc_n=16)

        cc_marks = []

        def kv_evac_rope(kv, fg, t_mark, get_ps, evacs):
            for fsub in range(2):
                g = 2 * fg + fsub
                pb = g % 2
                if kv == 0:
                    w_list = [("T", t_mark), ("V", pr.c["V"]), ("LY", pr.c["LY"])]
                    sc_k = pr.emit("vector", lambda e, fg=fg, fsub=fsub, g=g, pb=pb: e.tensor_scalar(
                        sgA[pb][:, :], get_ps(fg, fsub)[:, :],
                        bk_s[:, g:g + 1], None, ALU.add), waits=w_list, inc="V")
                    evacs[2 * fg + fsub] = sc_k
                    pr.emit("sync", lambda e, pb=pb: e.dma_start(
                        swp[pb][0:64, :], sgA[pb][64:128, :]),
                        waits=[("V", sc_k)], inc="LY", inc_n=16)
                    sw_mark = pr.emit("sync", lambda e, pb=pb: e.dma_start(
                        swp[pb][64:128, :], sgA[pb][0:64, :]), inc="LY", inc_n=16)
                    pr.emit("vector", lambda e, pb=pb: e.tensor_tensor(
                        tmpr[pb][:, :], swp[pb][:, :], skq_s[:, :], ALU.mult),
                        waits=[("LY", sw_mark), ("LY", ly_trig)])
                    pr.emit("vector", lambda e, pb=pb: e.tensor_tensor(
                        sgB2[pb][:, :], sgA[pb][:, :], ckq_s[:, :], ALU.mult))
                    v_r = pr.emit("vector", lambda e, pb=pb: e.tensor_tensor(
                        sgB2[pb][:, :], sgB2[pb][:, :], tmpr[pb][:, :], ALU.add), inc="V")
                    cc_marks.append(pr.emit("sync", lambda e, g=g, pb=pb: e.dma_start(
                        cc_in.ap()[:, g * TOK:(g + 1) * TOK], sgB2[pb][:, :]),
                        waits=[("V", v_r)], inc="QA", inc_n=16))
                else:
                    w_list = [("T", t_mark), ("V", pr.c["V"]), ("LY", pr.c["LY"]),
                              ("T", pr.c["T"])]
                    sc_v = pr.emit("vector", lambda e, fg=fg, fsub=fsub, g=g, pb=pb: e.tensor_scalar(
                        sgA[pb][:, :], get_ps(fg, fsub)[:, :],
                        bv_s[:, g:g + 1], None, ALU.add), waits=w_list, inc="V")
                    evacs[2 * fg + fsub] = sc_v
                    v_r = None
                    for tb in range(8):
                        w2 = [("V", sc_v), ("V", pr.c["V"])]
                        t_tp = pr.emit("tensor", lambda e, pb=pb, tb=tb: e.transpose(
                            tp_ps[tb % 2][:, :], sgA[pb][:, tb * P:(tb + 1) * P],
                            i128b_s[:, :]), waits=w2, inc="T")
                        v_r = pr.emit("vector", lambda e, pb=pb, tb=tb: e.tensor_copy(
                            sgB2[pb][:, tb * P:(tb + 1) * P], tp_ps[tb % 2][:, :]),
                            waits=[("T", t_tp)], inc="V")
                    cc_marks.append(pr.emit("sync", lambda e, g=g, pb=pb: e.dma_start(
                        cc_in.ap()[:, G * TOK + g * TOK:G * TOK + (g + 1) * TOK],
                        sgB2[pb][:, :]),
                        waits=[("V", v_r)], inc="QB", inc_n=16))

        evacsK = {}

        def k_first_waits(fg):
            if fg >= 2:
                return [("V", evacsK[2 * (fg - 2) + 1])]
            if fg == 0:
                return [("V", pr.c["V"])]
            return []

        stream_gemm(ws_k, lambda fg, fsub: kv4[(2 * fg + fsub) % 4],
                    None, k_first_waits,
                    True, lambda fg, t: kv_evac_rope(0, fg, t, lambda a, b: kv4[(2 * a + b) % 4], evacsK))
        ws_v = make_ws(wv_t, 4)
        for u, ku in ((0, 6), (1, 7), (2, 5)):
            ws_load(ws_v, u, gate=ws_k["t_unit"][ku])


        scope_k23.close()
        kv2b = ps(scope_p1b, "kv2b", [P, TOK])
        tp_ps = [ps(scope_p1b, f"tp{i}", [P, P], BF16) for i in range(2)]
        vt3 = [kv01[0], kv01[1], kv2b]
        evacsV = {}

        def v_first_waits(fg):
            if fg == 0:
                return [("V", evacsK[5])]
            if fg == 1:
                return [("V", evacsV[0])]
            return [("V", evacsV[2 * fg - 2])]

        stream_gemm(ws_v, lambda fg, fsub: vt3[(2 * fg + fsub) % 3],
                    None, v_first_waits,
                    True, lambda fg, t: kv_evac_rope(1, fg, t, lambda a, b: vt3[(2 * a + b) % 3], evacsV))

        pr.wait("gpsimd", "QA", pr.c["QA"])
        pr.wait("gpsimd", "QB", pr.c["QB"])
        cc_m = pr.emit("gpsimd", lambda e: e.collective_compute(
            "AllGather", ALU.bypass,
            replica_groups=[[0, 1], [2, 3], [4, 5], [6, 7]],
            ins=[cc_in.ap()], outs=[cc_out.ap()]), inc="CC")
        pr.wait("sync", "CC", cc_m)
        pr.wait("scalar", "CC", cc_m)
        for q in range(2):
            eng = "sync" if q == 0 else "scalar"
            sq = "LY" if q == 0 else "SQ"
            pr.emit(eng, lambda e, q=q: e.dma_start(
                kT[:, :].rearrange("p (g c two i) -> p g c two i", g=G, two=2, i=P)[:, :, :, q, :],
                cc_out.ap()[q * P:(q + 1) * P, 0:G * TOK].rearrange(
                    "p (g j i) -> p g j i", g=G, i=P)), inc=sq, inc_n=16)
            pr.emit(eng, lambda e, q=q: e.dma_start(
                vT[:, :].rearrange("p (g c two i) -> p g c two i", g=G, two=2, i=P)[:, :, :, q, :],
                cc_out.ap()[q * P:(q + 1) * P, G * TOK:2 * G * TOK].rearrange(
                    "p (g j i) -> p g j i", g=G, i=P)), inc=sq, inc_n=16)
        if tap_kt is not None:
            pr.emit("sync", lambda e: e.dma_start(tap_kt, kT[:, :]), inc="LY", inc_n=16)
        if tap_vt is not None:
            pr.emit("sync", lambda e: e.dma_start(tap_vt, vT[:, :]), inc="LY", inc_n=16)
        pr.flush("p1b_kv")
        pr.join_all()
        scope_p1b.close()
        scope_sg.close()
        scope_1ab.close()

        scope_p2 = ExitStack()
        PT = sb(scope_p2, "PT", [P, PT_W], BF16)
        qh = [sb(scope_p2, f"qh{i}", [P, TOK], BF16) for i in range(2)]
        rZb = sb(scope_p2, "rZb", [P, TOK], F32)
        cstg = sb(scope_p2, "cstg", [P, TOK], F32)
        mask_s = sb(scope_p2, "mask_s", [P, NKC * P], BF16)
        stp = [ps(scope_p2, f"stp{i}", [P, TOK]) for i in range(2)]
        ctx_ps = ps(scope_p2, "ctx_ps", [P, TOK])
        z_ps = ps(scope_p2, "z_ps", [P, TOK])

        mask_mark = pr.emit("sync", lambda e: e.dma_start(
            mask_s[:, :].rearrange("p (k c) -> p k c", k=NKC),
            mk_h.rearrange("k p c -> p k c")), inc="LY", inc_n=16)
        qh_ld = {0: pr.emit("sync", lambda e: e.dma_start(qh[0][:, :], qbuf[0:P, :]),
                            inc="LY", inc_n=16)}

        t_p1 = None
        for m in range(8):
            kc = 2 * m
            lhs = kT[:, kc * P:(kc + 1) * P].rearrange("p (a s) -> p a s", s=4)[:, :, 0]
            t_p1 = pr.emit("tensor", lambda e, m=m, lhs=lhs: e.matmul(
                stp[0][0:32, m * 64:(m + 1) * 64], lhs, qh[0][:, 0:64],
                start=(m == 0), stop=(m == 7)),
                waits=[("LY", qh_ld[0])] if m == 0 else [],
                inc="T" if m == 7 else None)
        v_m1 = pr.emit("vector", lambda e: e.reduce_max(
            mx1[:, :], stp[0][0:32, 0:512], axis=AX.X), waits=[("T", t_p1)], inc="V")
        t_m1 = pr.emit("tensor", lambda e: e.matmul(
            stp[0][0:1, 512:544], mx1[:, :], i128f_s[0:32, 0:32], start=True, stop=True),
            waits=[("V", v_m1)], inc="T")
        v_m2 = pr.emit("vector", lambda e: e.reduce_max(
            mxs[:, :], stp[0][0:1, 512:544], axis=AX.X, negate=True),
            waits=[("T", t_m1)], inc="V")
        t_m2 = pr.emit("tensor", lambda e: e.matmul(
            stp[0][:, 576:577], ones1f[:, :], mxs[:, :], start=True, stop=True),
            waits=[("V", v_m2)], inc="T")
        v_m3 = pr.emit("vector", lambda e: e.tensor_copy(mneg[:, :], stp[0][:, 576:577]),
                       waits=[("T", t_m2)], inc="V")

        t_ctx = {}
        t_pv = {}
        v_div = {}
        v_rcp = {}
        prev_exp = {14: v_m3, 15: v_m3}
        for h in range(H):
            g = h // 4
            if h + 1 < H:
                w_list = [("T", t_ctx[h - 1])] if h >= 1 else []
                qh_ld[h + 1] = pr.emit("sync", lambda e, h=h: e.dma_start(
                    qh[(h + 1) % 2][:, :], qbuf[(h + 1) * P:(h + 2) * P, :]),
                    waits=w_list, inc="LY", inc_n=16)
            sc_exp = {}
            pl_msk = {}
            t_st = {}

            def emit_ST(kc, h=h, g=g, sc_exp=sc_exp, t_st=t_st):
                w = W_KC[kc]
                q0 = Q0[kc]
                buf = stp[kc % 2]
                first = True
                for (c0, c1) in _pieces(0, w):
                    w_list = []
                    if first:
                        first = False
                        if kc >= 2:
                            w_list.append(("Sc", sc_exp[kc - 2]))
                        elif h > 0:
                            w_list.append(("Sc", prev_exp[14 + kc]))
                        else:
                            w_list.append(("V", v_m3))
                        if kc == 0:
                            w_list.append(("LY", qh_ld[h]))
                    last = (c1 == w)
                    r = pr.emit("tensor", lambda e, g=g, kc=kc, c0=c0, c1=c1, q0=q0, h=h: e.matmul(
                        buf[:, c0:c1],
                        kT[:, g * S + kc * P:g * S + (kc + 1) * P],
                        qh[h % 2][:, q0 + c0: q0 + c1],
                        start=True, stop=True),
                        waits=w_list, inc="T" if last else None)
                    if last:
                        t_st[kc] = r

            def emit_PV(kc, h=h, g=g, pl_msk=pl_msk):
                q0k = Q0[kc]
                for (c0, c1) in _pieces(q0k, TOK):
                    stop = (kc == 7) if c0 < 512 else (kc == NKC - 1)
                    w_list = [("Pl", pl_msk[kc])]
                    if kc == 0 and h >= 1 and c0 == q0k:
                        w_list.append(("V", v_div[h - 1]))
                    last_pv = (kc == NKC - 1 and c1 == TOK)
                    r = pr.emit("tensor", lambda e, g=g, kc=kc, c0=c0, c1=c1, q0k=q0k, stop=stop: e.matmul(
                        ctx_ps[:, c0:c1],
                        vT[:, g * S + kc * P:g * S + (kc + 1) * P],
                        PT[:, OFF_KC[kc] + (c0 - q0k): OFF_KC[kc] + (c1 - q0k)],
                        start=(kc == 0), stop=stop),
                        waits=w_list, inc="T" if last_pv else None)
                    if last_pv:
                        t_pv[h] = r

            def emit_exp(kc, h=h, sc_exp=sc_exp, t_st=t_st):
                w = W_KC[kc]
                w_list = [("T", t_st[kc])]
                if h == 0 and kc <= 1:
                    w_list.append(("V", v_m3))
                sc_exp[kc] = pr.emit("scalar", lambda e, w=w, kc=kc: e.activation(
                    PT[:, OFF_KC[kc]:OFF_KC[kc] + w], stp[kc % 2][:, 0:w],
                    AF.Exp, bias=mneg[:, :], scale=1.0), waits=w_list, inc="Sc")

            def emit_msk(kc, h=h, sc_exp=sc_exp, pl_msk=pl_msk):
                wm_ = [("Sc", sc_exp[kc])]
                if h == 0 and kc == 0:
                    wm_.append(("LY", mask_mark))
                pl_msk[kc] = pr.emit("gpsimd", lambda e, kc=kc: e.tensor_tensor(
                    PT[:, OFF_KC[kc]:OFF_KC[kc] + P],
                    PT[:, OFF_KC[kc]:OFF_KC[kc] + P],
                    mask_s[:, kc * P:(kc + 1) * P], ALU.mult), waits=wm_, inc="Pl")

            for kc in range(NKC):
                emit_ST(kc)
                emit_exp(kc)
                emit_msk(kc)
                if kc >= 2:
                    emit_PV(kc - 2)
            emit_PV(NKC - 2)
            emit_PV(NKC - 1)
            prev_exp = {14: sc_exp[14], 15: sc_exp[15]}
            first = True
            for kc in range(NKC):
                q0k = Q0[kc]
                for (c0, c1) in _pieces(q0k, TOK):
                    w_list = []
                    if first:
                        if h >= 1:
                            w_list.append(("V", v_rcp[h - 1]))
                        first = False
                    last = (kc == NKC - 1 and c1 == TOK)
                    r = pr.emit("tensor", lambda e, kc=kc, c0=c0, c1=c1, q0k=q0k: e.matmul(
                        z_ps[:, c0:c1], ones128b[:, :],
                        PT[:, OFF_KC[kc] + (c0 - q0k): OFF_KC[kc] + (c1 - q0k)],
                        start=(kc == 0), stop=(kc == NKC - 1)),
                        waits=w_list, inc="T" if last else None)
                    if last:
                        t_ctx[h] = r
            v_div[h] = pr.emit("vector", lambda e: e.tensor_copy(cstg[:, :], ctx_ps[:, :]),
                               waits=[("T", t_pv[h])], inc="V")
            v_rcp[h] = pr.emit("vector", lambda e: e.reciprocal(rZb[:, :], z_ps[:, :]),
                               waits=[("T", t_ctx[h])], inc="V")
            pr.emit("vector", lambda e, h=h: e.tensor_tensor(
                big0[:, h * TOK:(h + 1) * TOK], cstg[:, :], rZb[:, :], ALU.mult), inc="V")
        if tap_ctx is not None:
            pr.emit("sync", lambda e: e.dma_start(tap_ctx, big0[:, :]),
                    waits=[("V", pr.c["V"])], inc="LY", inc_n=16)
        pr.flush("p2_attn")
        pr.join_all()
        scope_p2.close()
        scope_kv.close()

        scope_p37 = ExitStack()
        WT["bufs"] = [sb(scope_p37, f"wtb{i}", [P, 4096], BF16) for i in range(4)]
        z1s = [sb(scope_p37, f"z1s{i}", [P, TOK], F32) for i in range(3)]
        xres = [sb(scope_p37, f"xres{i}", [P, TOK], F32) for i in range(3)]
        zbf = [sb(scope_p37, f"zbf{i}", [P, TOK], BF16) for i in range(2)]
        zsq = [sb(scope_p37, f"zsq{i}", [P, TOK], BF16) for i in range(2)]
        rb_s = sb(scope_p37, "rb_s", [P, TOK], F32)
        mrs_s = sb(scope_p37, "mrs_s", [P, TOK], F32)
        s_mean = sb(scope_p37, "s_mean", [1, TOK], F32)
        s_tb = sb(scope_p37, "s_tb", [1, TOK], F32)
        s_tc = sb(scope_p37, "s_tc", [1, TOK], F32)

        def gemm_ln_phase(ws, bias_sb, outbuf, resload, deltas, wp, st_ps):
            res_mark = {0: resload(0, []), 1: resload(1, [])}
            v_stt = {}
            sc_sq = {}
            pl_cp = {}
            st_mark = {}
            st_t = {}

            def first_waits(fg):
                if fg == 1:
                    return [("V", v_stt[0])]
                if fg >= 2:
                    return [("V", v_stt[2 * fg - 2])]
                return []

            def emit_stats(fc):
                pb = fc % 2
                for nh in range(2):
                    pr.emit("tensor", lambda e, pb=pb, nh=nh, fc=fc: e.matmul(
                        st_ps[0:1, nh * 512:(nh + 1) * 512], onescb[:, :],
                        zbf[pb][:, nh * 512:(nh + 1) * 512],
                        start=(fc == 0), stop=False),
                        waits=[("Sc", pl_cp[fc])] if nh == 0 else [], inc=None)
                    last = (nh == 1)
                    r = pr.emit("tensor", lambda e, pb=pb, nh=nh, fc=fc: e.matmul(
                        st_ps[0:2, nh * 512:(nh + 1) * 512], zo2[:, :],
                        zsq[pb][:, nh * 512:(nh + 1) * 512],
                        start=False, stop=(fc == 31)),
                        waits=[("Sc", sc_sq[fc])] if nh == 0 else [],
                        inc="T" if last else None)
                    if last:
                        st_t[fc] = r

            def fg_done(fg, t_mark):
                if deltas is not None:
                    t_mark = deltas(fg)
                if fg >= 1:
                    emit_stats(2 * (fg - 1))
                    emit_stats(2 * (fg - 1) + 1)
                for fsub in range(2):
                    fc = 2 * fg + fsub
                    pb = fc % 2
                    w_list = [("T", t_mark), res_mark[fc]]
                    if fc >= 2:
                        w_list += [("Sc", max(sc_sq[fc - 2], pl_cp[fc - 2])),
                                   st_mark[fc - 2]]
                    pr.emit("vector", lambda e, fsub=fsub, fc=fc, pb=pb: e.tensor_scalar(
                        z1s[pb][:, :], wp[fc % 3][:, :], bias_sb[:, fc:fc + 1], None,
                        ALU.add), waits=w_list)
                    v_stt[fc] = pr.emit("vector", lambda e, pb=pb: e.tensor_tensor(
                        z1s[pb][:, :], z1s[pb][:, :], xres[pb][:, :], ALU.add), inc="V")
                    w_sq = [("V", v_stt[fc])]
                    if fc >= 2:
                        w_sq.append(("T", st_t[fc - 2]))
                    sc_sq[fc] = pr.emit("scalar", lambda e, pb=pb: e.activation(
                        zsq[pb][:, :], z1s[pb][:, :], AF.Square),
                        waits=w_sq, inc="Sc")
                    w_cp = [("V", v_stt[fc])]
                    if fc >= 2:
                        w_cp.append(("T", st_t[fc - 2]))
                    pl_cp[fc] = pr.emit("scalar", lambda e, pb=pb: e.copy(
                        zbf[pb][:, :], z1s[pb][:, :]), waits=w_cp, inc="Sc")
                    ms = f"M{fc % 2}"
                    st_mark[fc] = (ms, pr.emit("scalar", lambda e, fc=fc, pb=pb, outbuf=outbuf: e.dma_start(
                        outbuf[fc * P:(fc + 1) * P, :], z1s[pb][:, :]),
                        waits=[("V", v_stt[fc])], inc=ms, inc_n=16))
                for fc2 in (2 * fg + 2, 2 * fg + 3):
                    if fc2 < 32:
                        res_mark[fc2] = resload(fc2, [("V", v_stt[fc2 - 2])])

            stream_gemm(ws, lambda fg, fsub: wp[(2 * fg + fsub) % 3], None, first_waits,
                        deltas is None, fg_done)
            emit_stats(30)
            emit_stats(31)
            return st_mark

        def ln_finalize(st_ps, rbc_ps):
            sc_c = pr.emit("scalar", lambda e: e.copy(st2_sb[:, :], st_ps[:, :]),
                           inc="Sc")
            ly_r = pr.emit("sync", lambda e: e.dma_start(s2row[:, :], st2_sb[1:2, :]),
                           waits=[("Sc", sc_c)], inc="LY", inc_n=16)
            v0 = pr.emit("vector", lambda e: e.tensor_scalar(
                s_mean[:, :], st2_sb[0:1, :], 1.0 / DM, None, ALU.mult),
                waits=[("Sc", sc_c)], inc="V")
            v0 = pr.emit("vector", lambda e: e.tensor_scalar(
                s_tb[:, :], s2row[:, :], 1.0 / DM, None, ALU.mult),
                waits=[("LY", ly_r)], inc="V")
            v0 = pr.emit("vector", lambda e: e.tensor_tensor(
                s_tc[:, :], s_mean[:, :], s_mean[:, :], ALU.mult), inc="V")
            v0 = pr.emit("vector", lambda e: e.tensor_tensor(
                s_tb[:, :], s_tb[:, :], s_tc[:, :], ALU.subtract), inc="V")
            v0 = pr.emit("vector", lambda e: e.tensor_scalar(
                s_tb[:, :], s_tb[:, :], 1e-5, None, ALU.add), inc="V")
            sc0 = pr.emit("scalar", lambda e: e.activation(
                s_tc[:, :], s_tb[:, :], AF.Sqrt),
                waits=[("V", v0)], inc="Sc")
            v1 = pr.emit("vector", lambda e: e.reciprocal(s_tb[:, :], s_tc[:, :]),
                         waits=[("Sc", sc0)], inc="V")
            t0 = None
            for nh in range(2):
                t0 = pr.emit("tensor", lambda e, nh=nh: e.matmul(
                    rbc_ps[:, nh * 512:(nh + 1) * 512], ones1f[:, :],
                    s_tb[0:1, nh * 512:(nh + 1) * 512], start=True, stop=True),
                    waits=[("V", v1)], inc="T" if nh == 1 else None)
            v2 = pr.emit("vector", lambda e: e.tensor_copy(rb_s[:, :], rbc_ps[:, :]),
                         waits=[("T", t0)], inc="V")
            v2 = pr.emit("vector", lambda e: e.tensor_tensor(
                s_tc[:, :], s_mean[:, :], s_tb[:, :], ALU.mult), inc="V")
            for nh in range(2):
                t0 = pr.emit("tensor", lambda e, nh=nh: e.matmul(
                    rbc_ps[:, nh * 512:(nh + 1) * 512], ones1f[:, :],
                    s_tc[0:1, nh * 512:(nh + 1) * 512], start=True, stop=True),
                    waits=[("V", v2)], inc="T" if nh == 1 else None)
            v2 = pr.emit("vector", lambda e: e.tensor_copy(mrs_s[:, :], rbc_ps[:, :]),
                         waits=[("T", t0)], inc="V")
            return v2

        def ln_normalize(inbuf, in_marks, to_big0, outbuf, v_fin):
            marks = {}
            rd = {}
            vv = {}
            pp = {}
            ptag = "Sc" if to_big0 else "V"
            for fc in range(3):
                ns = f"N{fc % 3}"
                rd[fc] = (ns, pr.emit("sync", lambda e, fc=fc, inbuf=inbuf: e.dma_start(
                    z1s[fc % 3][:, :], inbuf[fc * P:(fc + 1) * P, :]),
                    waits=[in_marks[fc]], inc=ns, inc_n=16))
            for fc in range(NCH):
                pb = fc % 3
                w_list = [rd[fc], ("V", v_fin)]
                if fc >= 3:
                    w_list.append((ptag, pp[fc - 3]))
                vv[fc] = pr.emit("vector", lambda e, pb=pb: e.tensor_tensor(
                    xres[pb][:, :], z1s[pb][:, :], rb_s[:, :], ALU.mult),
                    waits=w_list, inc="V")
                pp[fc] = pr.emit("vector", lambda e, pb=pb: e.tensor_tensor(
                    z1s[pb][:, :], xres[pb][:, :], mrs_s[:, :], ALU.subtract), inc="V")
                if to_big0:
                    pp[fc] = pr.emit("scalar", lambda e, pb=pb, fc=fc: e.copy(
                        big0[:, fc * TOK:(fc + 1) * TOK], z1s[pb][:, :]),
                        waits=[("V", pp[fc])], inc="Sc")
                ms = f"M{fc % 3}"
                marks[fc] = (ms, pr.emit("scalar", lambda e, fc=fc, pb=pb, outbuf=outbuf: e.dma_start(
                    outbuf[fc * P:(fc + 1) * P, :], z1s[pb][:, :]),
                    waits=[(ptag, pp[fc])], inc=ms, inc_n=16))
                if fc + 3 < NCH:
                    ns = f"N{fc % 3}"
                    rd[fc + 3] = (ns, pr.emit("sync", lambda e, fc=fc, pb=pb, inbuf=inbuf: e.dma_start(
                        z1s[pb][:, :], inbuf[(fc + 3) * P:(fc + 4) * P, :]),
                        waits=[in_marks[fc + 3], (ptag, pp[fc]), marks[fc]],
                        inc=ns, inc_n=16))
            return marks

        scope_s12a = ExitStack()
        sta_ps = ps(scope_s12a, "sta_ps", [2, TOK])
        scope_wp3 = ExitStack()
        wp3 = [ps(scope_wp3, f"wp3{i}", [P, TOK]) for i in range(3)]

        def p3_resload(fc, waits):
            rs = f"R{fc % 2}"
            return (rs, pr.emit("sync", lambda e, fc=fc: e.dma_start(
                xres[fc % 2][:, :], xsel[fc * P:(fc + 1) * P, :]),
                waits=waits, inc=rs, inc_n=16))

        ws_o = make_ws(wo_t, 16)
        z1_marks = gemm_ln_phase(ws_o, bo_s, z1buf, p3_resload, None, wp3, sta_ps)
        pr.flush("p3_wo")
        t_after_p3 = pr.c["T"]
        pr.join_all()
        scope_wp3.close()

        scope_p4 = ExitStack()
        rbc4 = ps(scope_p4, "rbc4", [P, TOK])
        v_fin1 = ln_finalize(sta_ps, rbc4)
        y1_marks = ln_normalize(z1buf, z1_marks, True, y1buf, v_fin1)
        ws_f = make_ws(kn_t, 16, init_gate=t_after_p3)
        for u in range(4):
            ws_load(ws_f, u)
        pr.flush("p4_ln1")
        pr.join_all()
        scope_p4.close()
        scope_s12a.close()

        scope_p5 = ExitStack()
        axp = ps(scope_p5, "axp", [P, TOK])
        t_ax = None
        for ch in range(NCH):
            w_list = [("LG", lg_asht)] if ch == 0 else []
            for nh in range(2):
                last = (ch == NCH - 1 and nh == 1)
                t_ax = pr.emit("tensor", lambda e, ch=ch, nh=nh: e.matmul(
                    axp[:, nh * 512:(nh + 1) * 512],
                    A_comb[:, ch * 128:(ch + 1) * 128],
                    big0[:, ch * TOK + nh * 512: ch * TOK + (nh + 1) * 512],
                    start=(ch == 0), stop=(ch == NCH - 1)),
                    waits=w_list if nh == 0 else [], inc="T" if last else None)
        sc_ax = pr.emit("scalar", lambda e: e.copy(Ax_sb[:, :], axp[:, :]),
                waits=[("T", t_ax)], inc="Sc")
        if tap_ax is not None:
            pr.emit("sync", lambda e: e.dma_start(tap_ax, Ax_sb[:, :]),
                    waits=[("Sc", sc_ax)], inc="LY", inc_n=16)
            pr.emit("sync", lambda e: e.dma_start(tap_acomb, A_comb[:, :]), inc="LY", inc_n=16)
            pr.emit("sync", lambda e: e.dma_start(tap_bstack, B_stack[:, :]), inc="LY", inc_n=16)
        pr.flush("p5_ax")
        pr.join_all()
        scope_p5.close()

        scope_p6 = ExitStack()
        stb_ps = ps(scope_p6, "stb_ps", [2, TOK])
        scope_wp6 = ExitStack()
        wp6 = [ps(scope_wp6, f"wp6{i}", [P, TOK]) for i in range(3)]

        def p6_resload(fc, waits):
            rs = f"R{fc % 2}"
            return (rs, pr.emit("sync", lambda e, fc=fc: e.dma_start(
                xres[fc % 2][:, :], y1buf[fc * P:(fc + 1) * P, :]),
                waits=list(waits) + [y1_marks[fc]], inc=rs, inc_n=16))

        def p6_deltas(fg):
            t_l = None
            first = True
            for fsub in range(2):
                fc = 2 * fg + fsub
                for half in range(2):
                    r0 = 0 if half == 0 else 64
                    w_list = [("LG", lg_bsht), ("LY", b_last_mark)] if (fg == 0 and first) else []
                    first = False
                    last = (fsub == 1 and half == 1)
                    t_l = pr.emit("tensor", lambda e, fc=fc, fsub=fsub, half=half, r0=r0: e.matmul(
                        wp6[fc % 3][:, half * 512:(half + 1) * 512],
                        B_stack[r0:r0 + 64, fc * P:(fc + 1) * P],
                        Ax_sb[r0:r0 + 64, half * 512:(half + 1) * 512],
                        start=False, stop=True),
                        waits=w_list, inc="T" if last else None)
            return t_l

        z2_marks = gemm_ln_phase(ws_f, bi_s, z2buf, p6_resload, p6_deltas, wp6, stb_ps)
        pr.flush("p6_ffn")
        pr.join_all()
        scope_wp6.close()

        rbc7 = ps(scope_p6, "rbc7", [P, TOK])
        v_fin2 = ln_finalize(stb_ps, rbc7)
        out_marks = ln_normalize(z2buf, z2_marks, False, outT, v_fin2)
        for ms in ("M0", "M1", "M2"):
            pr.wait("sync", ms, pr.c[ms])
        pr.flush("p7_out")
        scope_p6.close()
        scope_p37.close()
        scope_big.close()

    return nc



def _pretile(w):
    nout = w.shape[1]
    return np.ascontiguousarray(
        w.reshape(32, 128, nout // 256, 256).transpose(2, 1, 0, 3))


def _host_inputs(L0_attn_hidden_states, L0_attn_Wq, L0_attn_bq, L0_attn_Wk, L0_attn_bk,
                 L0_attn_Wv, L0_attn_bv, L0_attn_Wo, L0_attn_bo, L0_ffn_gate,
                 L0_ffn_kernel, L0_ffn_bias, L0_ffn_lora_A, L0_ffn_lora_B,
                 L0_ffn_lora_A_shared, L0_ffn_lora_B_shared):
    f32 = np.float32
    hid = np.asarray(L0_attn_hidden_states, f32)
    shared = {
        "wq_t": _pretile(np.asarray(L0_attn_Wq, f32)),
        "wk_t": _pretile(np.asarray(L0_attn_Wk, f32)),
        "wv_t": _pretile(np.asarray(L0_attn_Wv, f32)),
        "wo_t": _pretile(np.asarray(L0_attn_Wo, f32)),
        "kn_t": _pretile(np.asarray(L0_ffn_kernel, f32)),
        "bq_r": np.ascontiguousarray(np.asarray(L0_attn_bq, f32).reshape(32, 128).T),
        "bk_r": np.ascontiguousarray(np.asarray(L0_attn_bk, f32).reshape(8, 128).T),
        "bv_r": np.ascontiguousarray(np.asarray(L0_attn_bv, f32).reshape(8, 128).T),
        "bo_r": np.ascontiguousarray(np.asarray(L0_attn_bo, f32).reshape(32, 128).T),
        "bi_r": np.ascontiguousarray(np.asarray(L0_ffn_bias, f32).reshape(32, 128).T),
        "la2": np.ascontiguousarray(np.asarray(L0_ffn_lora_A, f32).reshape(512, DM)),
        "lbt": np.ascontiguousarray(
            np.asarray(L0_ffn_lora_B, f32).transpose(0, 2, 1)).reshape(512, DM),
        "asht": np.ascontiguousarray(np.asarray(L0_ffn_lora_A_shared, f32).T),
        "bsht": np.ascontiguousarray(np.asarray(L0_ffn_lora_B_shared, f32).T),
    }
    esel = np.zeros((16, 512), f32)
    for a in range(4):
        for pp in range(128):
            esel[4 * a + pp // 32, a * 128 + pp] = 1.0
    i32s = np.zeros((128, 32), f32)
    for pp in range(128):
        i32s[pp, pp % 32] = 1.0
    shared["esel_h"] = esel
    shared["i32s_h"] = i32s.astype(BF)
    shared["i128b_h"] = np.eye(128).astype(BF)
    shared["i128f_h"] = np.eye(128, dtype=f32)

    inv_freq = (1.0 / (10000.0 ** (np.arange(64, dtype=np.float64) / 64.0))).astype(f32)

    def dup_trig(ang, scale):
        c = np.cos(ang) * scale
        s = np.sin(ang) * scale
        C = np.concatenate([c, c], axis=0)
        Ss = np.concatenate([-s, s], axis=0)
        return C.astype(BF), Ss.astype(BF)



    tri = (np.arange(P)[:, None] <= np.arange(P)[None, :]).astype(f32)

    gate = np.asarray(L0_ffn_gate, f32)
    gT = np.ascontiguousarray(gate.T)

    in_maps = []
    for core in range(8):
        b, p = core // 2, core % 2
        xTb = np.ascontiguousarray(hid[b].T)
        own_cols = np.concatenate([np.arange((2 * j + p) * 128, (2 * j + p) * 128 + 128)
                                   for j in range(8)])
        xsel_a = np.ascontiguousarray(xTb[:, own_cols])
        pos_q = own_cols.astype(f32)
        ang_q = inv_freq[:, None] * pos_q[None, :]
        scale = 1.0 / math.sqrt(D)
        mk = np.zeros((NKC, P, P), f32)
        for kc in range(NKC):
            if p == 0:
                mk[kc] = tri if kc % 2 == 0 else 0.0
            else:
                mk[kc] = 1.0 if kc % 2 == 0 else tri
        m = dict(shared)
        m["xsel"] = xsel_a
        m["cq_h"], m["sq_h"] = dup_trig(ang_q, scale)
        m["ckq_h"], m["skq_h"] = dup_trig(ang_q, 1.0)
        m["mk_h"] = mk.astype(BF)
        m["g2"] = np.ascontiguousarray(gT[:, [2 * b, 2 * b + 1]])
        in_maps.append(m)
    return in_maps


def kernel(**inputs):
    in_maps = _host_inputs(**inputs)
    nc = build_nc()
    res = run_bass_kernel_spmd(nc, in_maps, core_ids=list(range(8)))
    out = np.empty((B, S, DM), np.float32)
    for core in range(8):
        b, p = core // 2, core % 2
        y = res.results[core]["outT"]
        yt = np.ascontiguousarray(y.T).reshape(8, 128, DM)
        out[b].reshape(16, 128, DM)[p::2] = yt
    return out


# version 26
# speedup vs baseline: 1.0155x; 1.0115x over previous
import math
from contextlib import ExitStack

import numpy as np
import ml_dtypes

import concourse.bass as bass
import concourse.mybir as mybir
from concourse.bass_utils import run_bass_kernel_spmd

F32 = mybir.dt.float32
BF16 = mybir.dt.bfloat16
AF = mybir.ActivationFunctionType
ALU = mybir.AluOpType
AX = mybir.AxisListType
BF = ml_dtypes.bfloat16

P = 128
DM = 4096
NCH = DM // P
S = 2048
B = 4
H = 32
G = 8
D = 128
TOK = 1024
NKC = S // P

ENGINES = ("tensor", "scalar", "vector", "gpsimd", "sync")

Q0 = [128 * (kc // 2) for kc in range(NKC)]
KPOS = [(kc % 2) * 8 + kc // 2 for kc in range(NKC)]
W_KC = [TOK - q0 for q0 in Q0]
OFF_KC = [int(x) for x in np.concatenate([[0], np.cumsum(W_KC)])]
PT_W = OFF_KC[-1]


def _pieces(c0, c1, align=512):
    out = []
    c = c0
    while c < c1:
        nxt = min(c1, (c // align + 1) * align)
        out.append((c, nxt))
        c = nxt
    return out


class Prog:

    def __init__(self, nc, sems):
        self.nc = nc
        self.sems = sems
        self.q = {e: [] for e in ENGINES}
        self.c = {k: 0 for k in sems}
        self.serialize = False

    def emit(self, eng, fn, waits=(), inc=None, inc_n=1):
        if self.serialize:
            waits = tuple(waits) + tuple((k, v) for k, v in self.c.items() if v > 0)
        waits = tuple(waits)
        for s, v in waits:
            assert v <= self.c[s], f"wait on {s}>={v} but only {self.c[s]} scheduled"
        sems = self.sems

        def run(e, fn=fn, waits=waits, inc=inc, inc_n=inc_n):
            for s, v in waits:
                e.wait_ge(sems[s], v)
            r = fn(e)
            if inc is not None:
                r.then_inc(sems[inc], inc_n)
        self.q[eng].append(run)
        if inc is not None:
            self.c[inc] += inc_n
        return self.c[inc] if inc is not None else None

    def join_all(self):
        skip = {"W0", "W1", "W2", "W3"}
        snap = [(k, v) for k, v in self.c.items() if v > 0 and k not in skip]
        for e in ENGINES:
            if e == "gpsimd":
                continue
            for k, v in snap:
                self.wait(e, k, v)

    def wait(self, eng, sname, val):
        assert val <= self.c[sname], f"wait on {sname}>={val} but only {self.c[sname]}"
        sems = self.sems
        self.q[eng].append(lambda e, s=sname, v=val: e.wait_ge(sems[s], v))

    def flush(self, name):
        if name != "p7_out":
            return
        with self.nc.named_scope(name), self.nc.Block(name) as block:
            for e in ENGINES:
                ops = self.q[e]
                if not ops:
                    continue
                getattr(block, e)(lambda engine, ops=tuple(ops): [op(engine) for op in ops])
        self.q = {e: [] for e in ENGINES}


def build_nc(taps=()):
    nc = bass.Bass("TRN2", target_bir_lowering=False)

    def dram_in(name, shape, dt):
        return nc.dram_tensor(name, shape, dt, kind="ExternalInput").ap()

    def dram_scratch(name, shape, dt):
        if name in taps:
            return nc.dram_tensor(name, shape, dt, kind="ExternalOutput").ap()
        return nc.dram_tensor(name, shape, dt).ap()

    xsel = dram_in("xsel", [DM, TOK], F32)
    wq_t = dram_in("wq_t", [16, P, 32, 256], F32)
    wk_t = dram_in("wk_t", [4, P, 32, 256], F32)
    wv_t = dram_in("wv_t", [4, P, 32, 256], F32)
    wo_t = dram_in("wo_t", [16, P, 32, 256], F32)
    kn_t = dram_in("kn_t", [16, P, 32, 256], F32)
    bq_r = dram_in("bq_r", [P, 32], F32)
    bk_r = dram_in("bk_r", [P, 8], F32)
    bv_r = dram_in("bv_r", [P, 8], F32)
    bo_r = dram_in("bo_r", [P, 32], F32)
    bi_r = dram_in("bi_r", [P, 32], F32)
    g2 = dram_in("g2", [16, 2], F32)
    la2 = dram_in("la2", [512, DM], F32)
    lbt = dram_in("lbt", [512, DM], F32)
    asht = dram_in("asht", [DM, 32], F32)
    bsht = dram_in("bsht", [32, DM], F32)
    cq_h = dram_in("cq_h", [P, TOK], BF16)
    sq_h = dram_in("sq_h", [P, TOK], BF16)
    ckq_h = dram_in("ckq_h", [P, TOK], BF16)
    skq_h = dram_in("skq_h", [P, TOK], BF16)
    mk_h = dram_in("mk_h", [NKC, P, P], BF16)
    esel_h = dram_in("esel_h", [16, 512], F32)
    i32s_h = dram_in("i32s_h", [P, 32], BF16)
    i128b_h = dram_in("i128b_h", [P, P], BF16)
    i128f_h = dram_in("i128f_h", [P, P], F32)

    qbuf = dram_scratch("qbuf", [DM, TOK], BF16)
    cc_in = nc.dram_tensor("cc_in", [P, 2 * G * TOK], BF16)
    cc_out = nc.dram_tensor("cc_out", [2 * P, 2 * G * TOK], BF16)
    z1buf = dram_scratch("z1buf", [DM, TOK], F32)
    y1buf = dram_scratch("y1buf", [DM, TOK], F32)
    z2buf = dram_scratch("z2buf", [DM, TOK], F32)
    outT = nc.dram_tensor("outT", [DM, TOK], F32, kind="ExternalOutput").ap()

    tap_kt = nc.dram_tensor("tap_kt", [P, G * S], BF16, kind="ExternalOutput").ap() if "kt" in taps else None
    tap_vt = nc.dram_tensor("tap_vt", [P, G * S], BF16, kind="ExternalOutput").ap() if "vt" in taps else None
    tap_ctx = nc.dram_tensor("tap_ctx", [P, NCH * TOK], BF16, kind="ExternalOutput").ap() if "ctx" in taps else None
    tap_ax = nc.dram_tensor("tap_ax", [P, TOK], BF16, kind="ExternalOutput").ap() if "ax" in taps else None
    tap_acomb = nc.dram_tensor("tap_acomb", [P, NCH * 128], BF16, kind="ExternalOutput").ap() if "ax" in taps else None
    tap_bstack = nc.dram_tensor("tap_bstack", [P, DM], BF16, kind="ExternalOutput").ap() if "ax" in taps else None

    stack = ExitStack()
    with stack:
        def sb(st, name, shape, dt):
            return st.enter_context(nc.sbuf_tensor(name, shape, dt))

        def ps(st, name, shape, dt=F32):
            return st.enter_context(nc.psum_tensor(name, shape, dt))

        bq_s = sb(stack, "bq_s", [P, 32], F32)
        bk_s = sb(stack, "bk_s", [P, 8], F32)
        bv_s = sb(stack, "bv_s", [P, 8], F32)
        bo_s = sb(stack, "bo_s", [P, 32], F32)
        bi_s = sb(stack, "bi_s", [P, 32], F32)
        g2_s = sb(stack, "g2_s", [16, 2], F32)
        esel_s = sb(stack, "esel_s", [16, 512], F32)
        i32s_s = sb(stack, "i32s_s", [P, 32], BF16)
        i128b_s = sb(stack, "i128b_s", [P, P], BF16)
        i128f_s = sb(stack, "i128f_s", [P, P], F32)
        ones128b = sb(stack, "ones128b", [P, P], BF16)
        ones1f = sb(stack, "ones1f", [1, P], F32)
        onescb = sb(stack, "onescb", [P, 1], BF16)
        zo2 = sb(stack, "zo2", [P, 2], BF16)
        st2_sb = sb(stack, "st2_sb", [2, TOK], F32)
        s2row = sb(stack, "s2row", [1, TOK], F32)
        A_comb = sb(stack, "A_comb", [P, NCH * 128], BF16)
        B_stack = sb(stack, "B_stack", [P, DM], BF16)
        Ax_sb = sb(stack, "Ax_sb", [P, TOK], BF16)
        gdelta_s = sb(stack, "gdelta_s", [P, 8 * 32], BF16)
        gr_sb = sb(stack, "gr_sb", [P, 1], F32)
        mneg = sb(stack, "mneg", [P, 1], F32)
        mx1 = sb(stack, "mx1", [32, 1], F32)
        mxs = sb(stack, "mxs", [1, 1], F32)

        sm = {k: stack.enter_context(nc.semaphore(k))
              for k in ("T", "Sc", "V", "Pl", "LG", "LY", "SQ", "CC",
                        "N0", "N1", "N2", "R0", "R1", "W0", "W1", "W2", "W3",
                        "X0", "X1", "X2", "X3", "M0", "M1", "M2", "QA", "QB")}
        pr = Prog(nc, sm)

        WT = {}
        import os
        _ser = os.environ.get("KSER", "")
        

        def make_ws(wdram, nfg, init_gate=None):
            return {"wdram": wdram, "nfg": nfg, "nbuf": len(WT["bufs"]),
                    "entry_gate": pr.c["T"] if init_gate is None else init_gate,
                    "loads": {}, "t_unit": {}}

        def ws_load(ws, u, gate=None):
            if u in ws["loads"]:
                return ws["loads"][u]
            nbuf = ws["nbuf"]
            if gate is not None:
                pr.wait("gpsimd", "T", gate)
            elif u >= nbuf:
                pr.wait("gpsimd", "T", ws["t_unit"][u - nbuf])
            else:
                pr.wait("gpsimd", "T", ws["entry_gate"])
            fg, hf = u // 2, u % 2
            wsem = f"W{u % nbuf}"
            mrk = pr.emit("gpsimd", lambda e, hf=hf, fg=fg, u=u, nbuf=nbuf, wdram=ws["wdram"]: e.dma_start(
                WT["bufs"][u % nbuf][:, :].rearrange("p (d c) -> p d c", c=256),
                wdram[fg, :, hf * 16:(hf + 1) * 16, :]),
                inc=wsem, inc_n=16)
            ws["loads"][u] = (wsem, mrk)
            return ws["loads"][u]

        def stream_gemm(ws, get_psum, rhs_marks, first_fg_waits,
                        plain_stop, on_fg_done, mid_cb=None):
            t_unit = ws["t_unit"]
            t_fg = {}
            nfg = ws["nfg"]
            nbuf = ws["nbuf"]
            for fg in range(nfg):
                for hf in range(2):
                    u = 2 * fg + hf
                    wsem, mrk = ws_load(ws, u)
                    if u == 1 and mid_cb is not None:
                        mid_cb()
                    for gi in range(2):
                        if gi == 0:
                            pr.wait("tensor", wsem, mrk)
                        if rhs_marks is not None:
                            rs, rv = rhs_marks[hf * 16 + gi * 8 + 7]
                            pr.wait("tensor", rs, rv)
                        if hf == 0 and gi == 0:
                            for s_, v_ in first_fg_waits(fg):
                                pr.wait("tensor", s_, v_)
                        for dd in range(gi * 8, gi * 8 + 8):
                            d = hf * 16 + dd
                            for fsub in range(2):
                                for nh in range(2):
                                    last_u = (dd == 15 and fsub == 1 and nh == 1)
                                    last_fg = (d == NCH - 1 and fsub == 1 and nh == 1)
                                    r = pr.emit("tensor", lambda e, fg=fg, u=u, dd=dd, d=d, fsub=fsub, nh=nh, nbuf=nbuf: e.matmul(
                                        get_psum(fg, fsub)[:, nh * 512:(nh + 1) * 512],
                                        WT["bufs"][u % nbuf][:, dd * 256 + fsub * 128: dd * 256 + fsub * 128 + 128],
                                        big0[:, d * TOK + nh * 512: d * TOK + (nh + 1) * 512],
                                        start=(d == 0),
                                        stop=(d == NCH - 1 and plain_stop)),
                                        inc="T" if last_u else None)
                                    if last_u:
                                        t_unit[u] = r
                                    if last_fg:
                                        t_fg[fg] = r
                on_fg_done(fg, t_fg[fg])
            return t_fg

        pr.serialize = "b0" in _ser
        ly_consts = None
        for dst, src in ((bq_s, bq_r), (bk_s, bk_r), (bv_s, bv_r), (bo_s, bo_r),
                         (bi_s, bi_r), (g2_s, g2), (esel_s, esel_h), (i32s_s, i32s_h),
                         (i128b_s, i128b_h), (i128f_s, i128f_h)):
            ly_consts = pr.emit("sync", lambda e, d=dst, s=src: e.dma_start(d[:, :], s),
                                inc="LY", inc_n=16)
        pr.emit("gpsimd", lambda e: e.memset(ones128b[:, :], 1.0))
        pr.emit("gpsimd", lambda e: e.memset(ones1f[:, :], 1.0))
        pr.emit("gpsimd", lambda e: e.memset(onescb[:, :], 1.0))
        pr.emit("gpsimd", lambda e: e.memset(zo2[:, 0:1], 0.0))
        pr.emit("gpsimd", lambda e: e.memset(zo2[:, 1:2], 1.0), inc="Pl")
        pr.emit("gpsimd", lambda e: e.dma_start(
            A_comb[:, :].rearrange("p (ch c) -> p ch c", c=128)[:, :, 32:64],
            asht.rearrange("(ch p) r -> p ch r", p=P)), inc="LG", inc_n=16)
        lg_asht = pr.emit("gpsimd", lambda e: e.dma_start(
            A_comb[:, :].rearrange("p (ch c) -> p ch c", c=128)[:, :, 96:128],
            asht.rearrange("(ch p) r -> p ch r", p=P)), inc="LG", inc_n=16)
        pr.emit("gpsimd", lambda e: e.dma_start(B_stack[32:64, :], bsht),
                inc="LG", inc_n=16)
        lg_bsht = pr.emit("gpsimd", lambda e: e.dma_start(B_stack[96:128, :], bsht),
                          inc="LG", inc_n=16)

        scope_big = ExitStack()
        big0 = sb(scope_big, "big0", [P, NCH * TOK], BF16)
        scope_kv = ExitStack()
        kT = sb(scope_kv, "kT", [P, G * S], BF16)
        vT = sb(scope_kv, "vT", [P, G * S], BF16)
        scope_1ab = ExitStack()
        WT["bufs"] = [sb(scope_1ab, f"wta{i}", [P, 4096], BF16) for i in range(3)]
        sgA = [sb(scope_1ab, f"sgA{i}", [P, TOK], BF16) for i in range(2)]

        scope_b0 = ExitStack()
        lw = [sb(scope_b0, f"lw{i}", [P, 2048], BF16) for i in range(4)]
        ae_sb = sb(scope_b0, "ae_sb", [32, 2048], BF16)
        gr_ps = ps(scope_b0, "gr_ps", [P, 1])
        ae_ps = ps(scope_b0, "ae_ps", [32, 2048])
        aet_ps = ps(scope_b0, "aet_ps", [P, 32], BF16)

        pr.serialize = True
        v_prev = 0
        for vb in range(2):
            for a in range(4):
                t_gr = pr.emit("tensor", lambda e, a=a, vb=vb: e.matmul(
                    gr_ps[:, :], esel_s[:, a * P:(a + 1) * P], g2_s[:, vb:vb + 1],
                    start=True, stop=True),
                    waits=[("LY", ly_consts), ("V", v_prev)], inc="T")
                v_gr = pr.emit("vector", lambda e: e.tensor_copy(gr_sb[:, :], gr_ps[:, :]),
                               waits=[("T", t_gr)], inc="V")
                v_prev = pr.emit("vector", lambda e, vb=vb, a=a: e.tensor_scalar(
                    gdelta_s[:, (vb * 4 + a) * 32:(vb * 4 + a + 1) * 32],
                    i32s_s[:, :], gr_sb[:, :], None, ALU.mult), inc="V")
        v_gd_done = v_prev
        pr.serialize = False

        b_last_mark = None
        for branch, srcdram in (("A", la2), ("B", lbt)):
            for half in range(2):
                lg_marks = []
                pr.wait("gpsimd", "T", pr.c["T"])
                for rc in range(4):
                    lg = pr.emit("gpsimd", lambda e, rc=rc, half=half, sd=srcdram: e.dma_start(
                        lw[rc][:, :], sd[rc * P:(rc + 1) * P, half * 2048:(half + 1) * 2048]),
                        inc=f"X{rc}", inc_n=16)
                    lg_marks.append((f"X{rc}", lg))
                for vb in range(2):
                    t_last = None
                    for rc in range(4):
                        for ccp in range(4):
                            w_list = [lg_marks[rc]]
                            if rc == 0:
                                w_list += [("V", v_gd_done), ("Sc", pr.c["Sc"])]
                            t_last = pr.emit("tensor", lambda e, vb=vb, rc=rc, ccp=ccp: e.matmul(
                                ae_ps[:, ccp * 512:(ccp + 1) * 512],
                                gdelta_s[:, (vb * 4 + rc) * 32:(vb * 4 + rc + 1) * 32],
                                lw[rc][:, ccp * 512:(ccp + 1) * 512],
                                start=(rc == 0), stop=(rc == 3)),
                                waits=w_list if ccp == 0 else [],
                                inc="T" if (rc == 3 and ccp == 3) else None)
                    if branch == "B":
                        r0 = 0 if vb == 0 else 64
                        w_c = [("T", t_last)]
                        if b_last_mark is not None:
                            w_c.append(("LY", b_last_mark))
                        sc_b = pr.emit("scalar", lambda e: e.copy(ae_sb[:, :], ae_ps[:, :]),
                                       waits=w_c, inc="Sc")
                        b_last_mark = pr.emit("sync", lambda e, r0=r0, half=half: e.dma_start(
                            B_stack[r0:r0 + 32, half * 2048:(half + 1) * 2048], ae_sb[:, :]),
                            waits=[("Sc", sc_b)], inc="LY", inc_n=16)
                    else:
                        sc_a = pr.emit("scalar", lambda e: e.copy(ae_sb[:, :], ae_ps[:, :]),
                                       waits=[("T", t_last)], inc="Sc")
                        voff = 0 if vb == 0 else 64
                        for chh in range(16):
                            ch = half * 16 + chh
                            t_tp = pr.emit("tensor", lambda e, chh=chh: e.transpose(
                                aet_ps[:, :], ae_sb[:, chh * P:(chh + 1) * P], i128b_s[0:32, 0:32]),
                                waits=[("Sc", sc_a), ("V", pr.c["V"])], inc="T")
                            pr.emit("vector", lambda e, ch=ch, voff=voff: e.tensor_copy(
                                A_comb[:, ch * 128 + voff: ch * 128 + voff + 32], aet_ps[:, :]),
                                waits=[("T", t_tp)], inc="V")
        pr.serialize = False

        lg_xfb = {}

        def _xsel_load(g):
            xs = f"X{g}"
            mk = pr.emit("gpsimd", lambda e, g=g: e.dma_start(
                big0[:, g * 8 * TOK:(g + 1) * 8 * TOK].rearrange(
                    "p (c n) -> p c n", c=8),
                xsel[g * 8 * P:(g + 1) * 8 * P, :].rearrange(
                    "(c p) n -> p c n", p=P)),
                inc=xs, inc_n=16)
            for d in range(8 * g, 8 * g + 8):
                lg_xfb[d] = (xs, mk)

        ws_q = make_ws(wq_t, 16)
        pr.wait("gpsimd", "T", pr.c["T"])
        _xsel_load(0)
        _xsel_load(1)
        ws_load(ws_q, 0)
        ws_load(ws_q, 1)
        _xsel_load(2)
        _xsel_load(3)
        ws_load(ws_q, 2)

        pr.flush("b0_prep")
        pr.join_all()
        scope_b0.close()
        scope_sg = ExitStack()
        swp = [sb(scope_sg, f"swp{i}", [P, TOK], BF16) for i in range(2)]
        tmpr = [sb(scope_sg, f"tmpr{i}", [P, TOK], BF16) for i in range(2)]

        scope_p1a = ExitStack()
        sgB = [sb(scope_p1a, f"sgB{i}", [P, TOK], BF16) for i in range(2)]
        cq_s = sb(scope_p1a, "cq_s", [P, TOK], BF16)
        sq_s = sb(scope_p1a, "sq_s", [P, TOK], BF16)
        qp = [ps(scope_p1a, f"qp{i}", [P, TOK]) for i in range(4)]

        ly_trig = None
        for dst, src in ((cq_s, cq_h), (sq_s, sq_h)):
            ly_trig = pr.emit("sync", lambda e, d=dst, s=src: e.dma_start(d[:, :], s),
                              inc="LY", inc_n=16)

        sc_evac = {}
        rope_done = {}
        sq_store = {}

        def p1a_first_waits(fg):
            if fg >= 2:
                return [("V", sc_evac[2 * (fg - 2) + 1])]
            return []

        def p1a_fg_done(fg, t_mark):
            for fsub in range(2):
                fc = 2 * fg + fsub
                pb = fc % 2
                w_list = [("T", t_mark)]
                if fc >= 2:
                    w_list.append(rope_done[fc - 2])
                sc_evac[fc] = pr.emit("vector", lambda e, fg=fg, fsub=fsub, fc=fc, pb=pb: e.tensor_scalar(
                    sgA[pb][:, :], qp[2 * (fg % 2) + fsub][:, :],
                    bq_s[:, fc:fc + 1], None, ALU.add), waits=w_list, inc="V")
                w_sw = [("V", sc_evac[fc])]
                if fc >= 2:
                    w_sw.append(rope_done[fc - 2])
                pr.emit("sync", lambda e, pb=pb: e.dma_start(
                    swp[pb][0:64, :], sgA[pb][64:128, :]), waits=w_sw, inc="LY", inc_n=16)
                sw_mark = pr.emit("sync", lambda e, pb=pb: e.dma_start(
                    swp[pb][64:128, :], sgA[pb][0:64, :]), inc="LY", inc_n=16)
                eng, esem = ("vector", "V")
                w_list = [("V", sc_evac[fc]), ("LY", sw_mark)]
                if fc < 2:
                    w_list.append(("LY", ly_trig))
                else:
                    w_list.append(sq_store[fc - 2])
                pr.emit(eng, lambda e, pb=pb: e.tensor_tensor(
                    tmpr[pb][:, :], swp[pb][:, :], sq_s[:, :], ALU.mult), waits=w_list)
                pr.emit(eng, lambda e, pb=pb: e.tensor_tensor(
                    sgB[pb][:, :], sgA[pb][:, :], cq_s[:, :], ALU.mult))
                rope_done[fc] = (esem, pr.emit(eng, lambda e, pb=pb: e.tensor_tensor(
                    sgB[pb][:, :], sgB[pb][:, :], tmpr[pb][:, :], ALU.add), inc=esem))
                qs = "QA" if fc % 2 == 0 else "QB"
                sq_store[fc] = (qs, pr.emit("sync", lambda e, fc=fc, pb=pb: e.dma_start(
                    qbuf[fc * P:(fc + 1) * P, :], sgB[pb][:, :]),
                    waits=[rope_done[fc]], inc=qs, inc_n=16))

        stream_gemm(ws_q, lambda fg, fsub: qp[2 * (fg % 2) + fsub],
                    lg_xfb, p1a_first_waits, True, p1a_fg_done)
        ws_k = make_ws(wk_t, 4)
        for u in range(3):
            ws_load(ws_k, u, gate=ws_q["t_unit"][29 + (u + 1) % 3])
        pr.flush("p1a_q")
        pr.join_all()
        scope_p1a.close()
        t_after_p1a = pr.c["T"]

        scope_p1b = ExitStack()
        ckq_s = sb(scope_p1b, "ckq_s", [P, TOK], BF16)
        skq_s = sb(scope_p1b, "skq_s", [P, TOK], BF16)
        sgB2 = [sb(scope_p1b, f"sgB2{i}", [P, TOK], BF16) for i in range(2)]
        kv01 = [ps(scope_p1b, f"kv{i}", [P, TOK]) for i in range(2)]
        scope_k23 = ExitStack()
        kv23 = [ps(scope_k23, f"kv{i+2}", [P, TOK]) for i in range(2)]
        kv4 = kv01 + kv23

        ly_trig = None
        for dst, src in ((ckq_s, ckq_h), (skq_s, skq_h)):
            ly_trig = pr.emit("sync", lambda e, d=dst, s=src: e.dma_start(d[:, :], s),
                              inc="LY", inc_n=16)

        cc_marks = []

        def kv_evac_rope(kv, fg, t_mark, get_ps, evacs):
            for fsub in range(2):
                g = 2 * fg + fsub
                pb = g % 2
                if kv == 0:
                    w_list = [("T", t_mark), ("V", pr.c["V"]), ("LY", pr.c["LY"])]
                    sc_k = pr.emit("vector", lambda e, fg=fg, fsub=fsub, g=g, pb=pb: e.tensor_scalar(
                        sgA[pb][:, :], get_ps(fg, fsub)[:, :],
                        bk_s[:, g:g + 1], None, ALU.add), waits=w_list, inc="V")
                    evacs[2 * fg + fsub] = sc_k
                    pr.emit("sync", lambda e, pb=pb: e.dma_start(
                        swp[pb][0:64, :], sgA[pb][64:128, :]),
                        waits=[("V", sc_k)], inc="LY", inc_n=16)
                    sw_mark = pr.emit("sync", lambda e, pb=pb: e.dma_start(
                        swp[pb][64:128, :], sgA[pb][0:64, :]), inc="LY", inc_n=16)
                    pr.emit("vector", lambda e, pb=pb: e.tensor_tensor(
                        tmpr[pb][:, :], swp[pb][:, :], skq_s[:, :], ALU.mult),
                        waits=[("LY", sw_mark), ("LY", ly_trig)])
                    pr.emit("vector", lambda e, pb=pb: e.tensor_tensor(
                        sgB2[pb][:, :], sgA[pb][:, :], ckq_s[:, :], ALU.mult))
                    v_r = pr.emit("vector", lambda e, pb=pb: e.tensor_tensor(
                        sgB2[pb][:, :], sgB2[pb][:, :], tmpr[pb][:, :], ALU.add), inc="V")
                    cc_marks.append(pr.emit("sync", lambda e, g=g, pb=pb: e.dma_start(
                        cc_in.ap()[:, g * TOK:(g + 1) * TOK], sgB2[pb][:, :]),
                        waits=[("V", v_r)], inc="QA", inc_n=16))
                else:
                    w_list = [("T", t_mark), ("V", pr.c["V"]), ("LY", pr.c["LY"]),
                              ("T", pr.c["T"])]
                    sc_v = pr.emit("vector", lambda e, fg=fg, fsub=fsub, g=g, pb=pb: e.tensor_scalar(
                        sgA[pb][:, :], get_ps(fg, fsub)[:, :],
                        bv_s[:, g:g + 1], None, ALU.add), waits=w_list, inc="V")
                    evacs[2 * fg + fsub] = sc_v
                    v_r = None
                    for tb in range(8):
                        w2 = [("V", sc_v), ("V", pr.c["V"])]
                        t_tp = pr.emit("tensor", lambda e, pb=pb, tb=tb: e.transpose(
                            tp_ps[tb % 2][:, :], sgA[pb][:, tb * P:(tb + 1) * P],
                            i128b_s[:, :]), waits=w2, inc="T")
                        v_r = pr.emit("vector", lambda e, pb=pb, tb=tb: e.tensor_copy(
                            sgB2[pb][:, tb * P:(tb + 1) * P], tp_ps[tb % 2][:, :]),
                            waits=[("T", t_tp)], inc="V")
                    cc_marks.append(pr.emit("sync", lambda e, g=g, pb=pb: e.dma_start(
                        cc_in.ap()[:, G * TOK + g * TOK:G * TOK + (g + 1) * TOK],
                        sgB2[pb][:, :]),
                        waits=[("V", v_r)], inc="QB", inc_n=16))

        evacsK = {}

        def k_first_waits(fg):
            if fg >= 2:
                return [("V", evacsK[2 * (fg - 2) + 1])]
            if fg == 0:
                return [("V", pr.c["V"])]
            return []

        stream_gemm(ws_k, lambda fg, fsub: kv4[(2 * fg + fsub) % 4],
                    None, k_first_waits,
                    True, lambda fg, t: kv_evac_rope(0, fg, t, lambda a, b: kv4[(2 * a + b) % 4], evacsK))
        ws_v = make_ws(wv_t, 4)
        for u, ku in ((0, 6), (1, 7), (2, 5)):
            ws_load(ws_v, u, gate=ws_k["t_unit"][ku])


        scope_k23.close()
        kv2b = ps(scope_p1b, "kv2b", [P, TOK])
        tp_ps = [ps(scope_p1b, f"tp{i}", [P, P], BF16) for i in range(2)]
        vt3 = [kv01[0], kv01[1], kv2b]
        evacsV = {}

        def v_first_waits(fg):
            if fg == 0:
                return [("V", evacsK[5])]
            if fg == 1:
                return [("V", evacsV[0])]
            return [("V", evacsV[2 * fg - 2])]

        stream_gemm(ws_v, lambda fg, fsub: vt3[(2 * fg + fsub) % 3],
                    None, v_first_waits,
                    True, lambda fg, t: kv_evac_rope(1, fg, t, lambda a, b: vt3[(2 * a + b) % 3], evacsV))

        pr.wait("gpsimd", "QA", pr.c["QA"])
        pr.wait("gpsimd", "QB", pr.c["QB"])
        cc_m = pr.emit("gpsimd", lambda e: e.collective_compute(
            "AllGather", ALU.bypass,
            replica_groups=[[0, 1], [2, 3], [4, 5], [6, 7]],
            ins=[cc_in.ap()], outs=[cc_out.ap()]), inc="CC")
        pr.wait("sync", "CC", cc_m)
        pr.wait("scalar", "CC", cc_m)
        for q in range(2):
            eng = "sync" if q == 0 else "scalar"
            sq = "LY" if q == 0 else "SQ"
            pr.emit(eng, lambda e, q=q: e.dma_start(
                kT[:, :].rearrange("p (g m t) -> p g m t", g=G, m=2)[:, :, q, :],
                cc_out.ap()[q * P:(q + 1) * P, 0:G * TOK].rearrange(
                    "p (g t) -> p g t", g=G)), inc=sq, inc_n=16)
            pr.emit(eng, lambda e, q=q: e.dma_start(
                vT[:, :].rearrange("p (g m t) -> p g m t", g=G, m=2)[:, :, q, :],
                cc_out.ap()[q * P:(q + 1) * P, G * TOK:2 * G * TOK].rearrange(
                    "p (g t) -> p g t", g=G)), inc=sq, inc_n=16)
        if tap_kt is not None:
            pr.emit("sync", lambda e: e.dma_start(tap_kt, kT[:, :]), inc="LY", inc_n=16)
        if tap_vt is not None:
            pr.emit("sync", lambda e: e.dma_start(tap_vt, vT[:, :]), inc="LY", inc_n=16)
        pr.flush("p1b_kv")
        pr.join_all()
        scope_p1b.close()
        scope_sg.close()
        scope_1ab.close()

        scope_p2 = ExitStack()
        PT = sb(scope_p2, "PT", [P, PT_W], BF16)
        qh = [sb(scope_p2, f"qh{i}", [P, TOK], BF16) for i in range(2)]
        rZb = sb(scope_p2, "rZb", [P, TOK], F32)
        cstg = sb(scope_p2, "cstg", [P, TOK], F32)
        mask_s = sb(scope_p2, "mask_s", [P, NKC * P], BF16)
        stp = [ps(scope_p2, f"stp{i}", [P, TOK]) for i in range(2)]
        ctx_ps = ps(scope_p2, "ctx_ps", [P, TOK])
        z_ps = ps(scope_p2, "z_ps", [P, TOK])

        mask_mark = pr.emit("sync", lambda e: e.dma_start(
            mask_s[:, :].rearrange("p (k c) -> p k c", k=NKC),
            mk_h.rearrange("k p c -> p k c")), inc="LY", inc_n=16)
        qh_ld = {0: pr.emit("sync", lambda e: e.dma_start(qh[0][:, :], qbuf[0:P, :]),
                            inc="LY", inc_n=16)}

        t_p1 = None
        for m in range(8):
            kc = 2 * m
            lhs = kT[:, KPOS[kc] * P:(KPOS[kc] + 1) * P].rearrange("p (a s) -> p a s", s=4)[:, :, 0]
            t_p1 = pr.emit("tensor", lambda e, m=m, lhs=lhs: e.matmul(
                stp[0][0:32, m * 64:(m + 1) * 64], lhs, qh[0][:, 0:64],
                start=(m == 0), stop=(m == 7)),
                waits=[("LY", qh_ld[0])] if m == 0 else [],
                inc="T" if m == 7 else None)
        v_m1 = pr.emit("vector", lambda e: e.reduce_max(
            mx1[:, :], stp[0][0:32, 0:512], axis=AX.X), waits=[("T", t_p1)], inc="V")
        t_m1 = pr.emit("tensor", lambda e: e.matmul(
            stp[0][0:1, 512:544], mx1[:, :], i128f_s[0:32, 0:32], start=True, stop=True),
            waits=[("V", v_m1)], inc="T")
        v_m2 = pr.emit("vector", lambda e: e.reduce_max(
            mxs[:, :], stp[0][0:1, 512:544], axis=AX.X, negate=True),
            waits=[("T", t_m1)], inc="V")
        t_m2 = pr.emit("tensor", lambda e: e.matmul(
            stp[0][:, 576:577], ones1f[:, :], mxs[:, :], start=True, stop=True),
            waits=[("V", v_m2)], inc="T")
        v_m3 = pr.emit("vector", lambda e: e.tensor_copy(mneg[:, :], stp[0][:, 576:577]),
                       waits=[("T", t_m2)], inc="V")

        t_ctx = {}
        t_pv = {}
        v_div = {}
        v_rcp = {}
        prev_exp = {14: v_m3, 15: v_m3}
        for h in range(H):
            g = h // 4
            if h + 1 < H:
                w_list = [("T", t_ctx[h - 1])] if h >= 1 else []
                qh_ld[h + 1] = pr.emit("sync", lambda e, h=h: e.dma_start(
                    qh[(h + 1) % 2][:, :], qbuf[(h + 1) * P:(h + 2) * P, :]),
                    waits=w_list, inc="LY", inc_n=16)
            sc_exp = {}
            pl_msk = {}
            t_st = {}

            def emit_ST(kc, h=h, g=g, sc_exp=sc_exp, t_st=t_st):
                w = W_KC[kc]
                q0 = Q0[kc]
                buf = stp[kc % 2]
                first = True
                for (c0, c1) in _pieces(0, w):
                    w_list = []
                    if first:
                        first = False
                        if kc >= 2:
                            w_list.append(("Sc", sc_exp[kc - 2]))
                        elif h > 0:
                            w_list.append(("Sc", prev_exp[14 + kc]))
                        else:
                            w_list.append(("V", v_m3))
                        if kc == 0:
                            w_list.append(("LY", qh_ld[h]))
                    last = (c1 == w)
                    r = pr.emit("tensor", lambda e, g=g, kc=kc, c0=c0, c1=c1, q0=q0, h=h: e.matmul(
                        buf[:, c0:c1],
                        kT[:, g * S + KPOS[kc] * P:g * S + (KPOS[kc] + 1) * P],
                        qh[h % 2][:, q0 + c0: q0 + c1],
                        start=True, stop=True),
                        waits=w_list, inc="T" if last else None)
                    if last:
                        t_st[kc] = r

            def emit_PV(kc, h=h, g=g, pl_msk=pl_msk):
                q0k = Q0[kc]
                for (c0, c1) in _pieces(q0k, TOK):
                    stop = (kc == 7) if c0 < 512 else (kc == NKC - 1)
                    w_list = [("Pl", pl_msk[kc])]
                    if kc == 0 and h >= 1 and c0 == q0k:
                        w_list.append(("V", v_div[h - 1]))
                    last_pv = (kc == NKC - 1 and c1 == TOK)
                    r = pr.emit("tensor", lambda e, g=g, kc=kc, c0=c0, c1=c1, q0k=q0k, stop=stop: e.matmul(
                        ctx_ps[:, c0:c1],
                        vT[:, g * S + KPOS[kc] * P:g * S + (KPOS[kc] + 1) * P],
                        PT[:, OFF_KC[kc] + (c0 - q0k): OFF_KC[kc] + (c1 - q0k)],
                        start=(kc == 0), stop=stop),
                        waits=w_list, inc="T" if last_pv else None)
                    if last_pv:
                        t_pv[h] = r

            def emit_exp(kc, h=h, sc_exp=sc_exp, t_st=t_st):
                w = W_KC[kc]
                w_list = [("T", t_st[kc])]
                if h == 0 and kc <= 1:
                    w_list.append(("V", v_m3))
                sc_exp[kc] = pr.emit("scalar", lambda e, w=w, kc=kc: e.activation(
                    PT[:, OFF_KC[kc]:OFF_KC[kc] + w], stp[kc % 2][:, 0:w],
                    AF.Exp, bias=mneg[:, :], scale=1.0), waits=w_list, inc="Sc")

            def emit_msk(kc, h=h, sc_exp=sc_exp, pl_msk=pl_msk):
                wm_ = [("Sc", sc_exp[kc])]
                if h == 0 and kc == 0:
                    wm_.append(("LY", mask_mark))
                pl_msk[kc] = pr.emit("gpsimd", lambda e, kc=kc: e.tensor_tensor(
                    PT[:, OFF_KC[kc]:OFF_KC[kc] + P],
                    PT[:, OFF_KC[kc]:OFF_KC[kc] + P],
                    mask_s[:, kc * P:(kc + 1) * P], ALU.mult), waits=wm_, inc="Pl")

            for kc in range(NKC):
                emit_ST(kc)
                emit_exp(kc)
                emit_msk(kc)
                if kc >= 2:
                    emit_PV(kc - 2)
            emit_PV(NKC - 2)
            emit_PV(NKC - 1)
            prev_exp = {14: sc_exp[14], 15: sc_exp[15]}
            first = True
            for kc in range(NKC):
                q0k = Q0[kc]
                for (c0, c1) in _pieces(q0k, TOK):
                    w_list = []
                    if first:
                        if h >= 1:
                            w_list.append(("V", v_rcp[h - 1]))
                        first = False
                    last = (kc == NKC - 1 and c1 == TOK)
                    r = pr.emit("tensor", lambda e, kc=kc, c0=c0, c1=c1, q0k=q0k: e.matmul(
                        z_ps[:, c0:c1], ones128b[:, :],
                        PT[:, OFF_KC[kc] + (c0 - q0k): OFF_KC[kc] + (c1 - q0k)],
                        start=(kc == 0), stop=(kc == NKC - 1)),
                        waits=w_list, inc="T" if last else None)
                    if last:
                        t_ctx[h] = r
            v_div[h] = pr.emit("vector", lambda e: e.tensor_copy(cstg[:, :], ctx_ps[:, :]),
                               waits=[("T", t_pv[h])], inc="V")
            v_rcp[h] = pr.emit("vector", lambda e: e.reciprocal(rZb[:, :], z_ps[:, :]),
                               waits=[("T", t_ctx[h])], inc="V")
            pr.emit("vector", lambda e, h=h: e.tensor_tensor(
                big0[:, h * TOK:(h + 1) * TOK], cstg[:, :], rZb[:, :], ALU.mult), inc="V")
        if tap_ctx is not None:
            pr.emit("sync", lambda e: e.dma_start(tap_ctx, big0[:, :]),
                    waits=[("V", pr.c["V"])], inc="LY", inc_n=16)
        pr.flush("p2_attn")
        pr.join_all()
        scope_p2.close()
        scope_kv.close()

        scope_p37 = ExitStack()
        WT["bufs"] = [sb(scope_p37, f"wtb{i}", [P, 4096], BF16) for i in range(4)]
        z1s = [sb(scope_p37, f"z1s{i}", [P, TOK], F32) for i in range(3)]
        xres = [sb(scope_p37, f"xres{i}", [P, TOK], F32) for i in range(3)]
        zbf = [sb(scope_p37, f"zbf{i}", [P, TOK], BF16) for i in range(2)]
        zsq = [sb(scope_p37, f"zsq{i}", [P, TOK], BF16) for i in range(2)]
        rb_s = sb(scope_p37, "rb_s", [P, TOK], F32)
        mrs_s = sb(scope_p37, "mrs_s", [P, TOK], F32)
        s_mean = sb(scope_p37, "s_mean", [1, TOK], F32)
        s_tb = sb(scope_p37, "s_tb", [1, TOK], F32)
        s_tc = sb(scope_p37, "s_tc", [1, TOK], F32)

        def gemm_ln_phase(ws, bias_sb, outbuf, resload, deltas, wp, st_ps):
            res_mark = {0: resload(0, []), 1: resload(1, [])}
            v_stt = {}
            sc_sq = {}
            pl_cp = {}
            st_mark = {}
            st_t = {}

            def first_waits(fg):
                if fg == 1:
                    return [("V", v_stt[0])]
                if fg >= 2:
                    return [("V", v_stt[2 * fg - 2])]
                return []

            def emit_stats(fc):
                pb = fc % 2
                for nh in range(2):
                    pr.emit("tensor", lambda e, pb=pb, nh=nh, fc=fc: e.matmul(
                        st_ps[0:1, nh * 512:(nh + 1) * 512], onescb[:, :],
                        zbf[pb][:, nh * 512:(nh + 1) * 512],
                        start=(fc == 0), stop=False),
                        waits=[("Sc", pl_cp[fc])] if nh == 0 else [], inc=None)
                    last = (nh == 1)
                    r = pr.emit("tensor", lambda e, pb=pb, nh=nh, fc=fc: e.matmul(
                        st_ps[0:2, nh * 512:(nh + 1) * 512], zo2[:, :],
                        zsq[pb][:, nh * 512:(nh + 1) * 512],
                        start=False, stop=(fc == 31)),
                        waits=[("Sc", sc_sq[fc])] if nh == 0 else [],
                        inc="T" if last else None)
                    if last:
                        st_t[fc] = r

            def fg_done(fg, t_mark):
                if deltas is not None:
                    t_mark = deltas(fg)
                if fg >= 1:
                    emit_stats(2 * (fg - 1))
                    emit_stats(2 * (fg - 1) + 1)
                for fsub in range(2):
                    fc = 2 * fg + fsub
                    pb = fc % 2
                    w_list = [("T", t_mark), res_mark[fc]]
                    if fc >= 2:
                        w_list += [("Sc", max(sc_sq[fc - 2], pl_cp[fc - 2])),
                                   st_mark[fc - 2]]
                    pr.emit("vector", lambda e, fsub=fsub, fc=fc, pb=pb: e.tensor_scalar(
                        z1s[pb][:, :], wp[fc % 3][:, :], bias_sb[:, fc:fc + 1], None,
                        ALU.add), waits=w_list)
                    v_stt[fc] = pr.emit("vector", lambda e, pb=pb: e.tensor_tensor(
                        z1s[pb][:, :], z1s[pb][:, :], xres[pb][:, :], ALU.add), inc="V")
                    w_sq = [("V", v_stt[fc])]
                    if fc >= 2:
                        w_sq.append(("T", st_t[fc - 2]))
                    sc_sq[fc] = pr.emit("scalar", lambda e, pb=pb: e.activation(
                        zsq[pb][:, :], z1s[pb][:, :], AF.Square),
                        waits=w_sq, inc="Sc")
                    w_cp = [("V", v_stt[fc])]
                    if fc >= 2:
                        w_cp.append(("T", st_t[fc - 2]))
                    pl_cp[fc] = pr.emit("scalar", lambda e, pb=pb: e.copy(
                        zbf[pb][:, :], z1s[pb][:, :]), waits=w_cp, inc="Sc")
                    ms = f"M{fc % 2}"
                    st_mark[fc] = (ms, pr.emit("scalar", lambda e, fc=fc, pb=pb, outbuf=outbuf: e.dma_start(
                        outbuf[fc * P:(fc + 1) * P, :], z1s[pb][:, :]),
                        waits=[("V", v_stt[fc])], inc=ms, inc_n=16))
                for fc2 in (2 * fg + 2, 2 * fg + 3):
                    if fc2 < 32:
                        res_mark[fc2] = resload(fc2, [("V", v_stt[fc2 - 2])])

            stream_gemm(ws, lambda fg, fsub: wp[(2 * fg + fsub) % 3], None, first_waits,
                        deltas is None, fg_done)
            emit_stats(30)
            emit_stats(31)
            return st_mark

        def ln_finalize(st_ps, rbc_ps):
            sc_c = pr.emit("scalar", lambda e: e.copy(st2_sb[:, :], st_ps[:, :]),
                           inc="Sc")
            ly_r = pr.emit("sync", lambda e: e.dma_start(s2row[:, :], st2_sb[1:2, :]),
                           waits=[("Sc", sc_c)], inc="LY", inc_n=16)
            v0 = pr.emit("vector", lambda e: e.tensor_scalar(
                s_mean[:, :], st2_sb[0:1, :], 1.0 / DM, None, ALU.mult),
                waits=[("Sc", sc_c)], inc="V")
            v0 = pr.emit("vector", lambda e: e.tensor_scalar(
                s_tb[:, :], s2row[:, :], 1.0 / DM, None, ALU.mult),
                waits=[("LY", ly_r)], inc="V")
            v0 = pr.emit("vector", lambda e: e.tensor_tensor(
                s_tc[:, :], s_mean[:, :], s_mean[:, :], ALU.mult), inc="V")
            v0 = pr.emit("vector", lambda e: e.tensor_tensor(
                s_tb[:, :], s_tb[:, :], s_tc[:, :], ALU.subtract), inc="V")
            v0 = pr.emit("vector", lambda e: e.tensor_scalar(
                s_tb[:, :], s_tb[:, :], 1e-5, None, ALU.add), inc="V")
            sc0 = pr.emit("scalar", lambda e: e.activation(
                s_tc[:, :], s_tb[:, :], AF.Sqrt),
                waits=[("V", v0)], inc="Sc")
            v1 = pr.emit("vector", lambda e: e.reciprocal(s_tb[:, :], s_tc[:, :]),
                         waits=[("Sc", sc0)], inc="V")
            t0 = None
            for nh in range(2):
                t0 = pr.emit("tensor", lambda e, nh=nh: e.matmul(
                    rbc_ps[:, nh * 512:(nh + 1) * 512], ones1f[:, :],
                    s_tb[0:1, nh * 512:(nh + 1) * 512], start=True, stop=True),
                    waits=[("V", v1)], inc="T" if nh == 1 else None)
            v2 = pr.emit("vector", lambda e: e.tensor_copy(rb_s[:, :], rbc_ps[:, :]),
                         waits=[("T", t0)], inc="V")
            v2 = pr.emit("vector", lambda e: e.tensor_tensor(
                s_tc[:, :], s_mean[:, :], s_tb[:, :], ALU.mult), inc="V")
            for nh in range(2):
                t0 = pr.emit("tensor", lambda e, nh=nh: e.matmul(
                    rbc_ps[:, nh * 512:(nh + 1) * 512], ones1f[:, :],
                    s_tc[0:1, nh * 512:(nh + 1) * 512], start=True, stop=True),
                    waits=[("V", v2)], inc="T" if nh == 1 else None)
            v2 = pr.emit("vector", lambda e: e.tensor_copy(mrs_s[:, :], rbc_ps[:, :]),
                         waits=[("T", t0)], inc="V")
            return v2

        def ln_normalize(inbuf, in_marks, to_big0, outbuf, v_fin):
            marks = {}
            rd = {}
            vv = {}
            pp = {}
            ptag = "Sc" if to_big0 else "V"
            for fc in range(3):
                ns = f"N{fc % 3}"
                rd[fc] = (ns, pr.emit("sync", lambda e, fc=fc, inbuf=inbuf: e.dma_start(
                    z1s[fc % 3][:, :], inbuf[fc * P:(fc + 1) * P, :]),
                    waits=[in_marks[fc]], inc=ns, inc_n=16))
            for fc in range(NCH):
                pb = fc % 3
                w_list = [rd[fc], ("V", v_fin)]
                if fc >= 3:
                    w_list.append((ptag, pp[fc - 3]))
                vv[fc] = pr.emit("vector", lambda e, pb=pb: e.tensor_tensor(
                    xres[pb][:, :], z1s[pb][:, :], rb_s[:, :], ALU.mult),
                    waits=w_list, inc="V")
                pp[fc] = pr.emit("vector", lambda e, pb=pb: e.tensor_tensor(
                    z1s[pb][:, :], xres[pb][:, :], mrs_s[:, :], ALU.subtract), inc="V")
                if to_big0:
                    pp[fc] = pr.emit("scalar", lambda e, pb=pb, fc=fc: e.copy(
                        big0[:, fc * TOK:(fc + 1) * TOK], z1s[pb][:, :]),
                        waits=[("V", pp[fc])], inc="Sc")
                ms = f"M{fc % 3}"
                marks[fc] = (ms, pr.emit("scalar", lambda e, fc=fc, pb=pb, outbuf=outbuf: e.dma_start(
                    outbuf[fc * P:(fc + 1) * P, :], z1s[pb][:, :]),
                    waits=[(ptag, pp[fc])], inc=ms, inc_n=16))
                if fc + 3 < NCH:
                    ns = f"N{fc % 3}"
                    rd[fc + 3] = (ns, pr.emit("sync", lambda e, fc=fc, pb=pb, inbuf=inbuf: e.dma_start(
                        z1s[pb][:, :], inbuf[(fc + 3) * P:(fc + 4) * P, :]),
                        waits=[in_marks[fc + 3], (ptag, pp[fc]), marks[fc]],
                        inc=ns, inc_n=16))
            return marks

        scope_s12a = ExitStack()
        sta_ps = ps(scope_s12a, "sta_ps", [2, TOK])
        scope_wp3 = ExitStack()
        wp3 = [ps(scope_wp3, f"wp3{i}", [P, TOK]) for i in range(3)]

        def p3_resload(fc, waits):
            rs = f"R{fc % 2}"
            return (rs, pr.emit("sync", lambda e, fc=fc: e.dma_start(
                xres[fc % 2][:, :], xsel[fc * P:(fc + 1) * P, :]),
                waits=waits, inc=rs, inc_n=16))

        ws_o = make_ws(wo_t, 16)
        z1_marks = gemm_ln_phase(ws_o, bo_s, z1buf, p3_resload, None, wp3, sta_ps)
        pr.flush("p3_wo")
        t_after_p3 = pr.c["T"]
        pr.join_all()
        scope_wp3.close()

        scope_p4 = ExitStack()
        rbc4 = ps(scope_p4, "rbc4", [P, TOK])
        v_fin1 = ln_finalize(sta_ps, rbc4)
        y1_marks = ln_normalize(z1buf, z1_marks, True, y1buf, v_fin1)
        ws_f = make_ws(kn_t, 16, init_gate=t_after_p3)
        for u in range(4):
            ws_load(ws_f, u)
        pr.flush("p4_ln1")
        pr.join_all()
        scope_p4.close()
        scope_s12a.close()

        scope_p5 = ExitStack()
        axp = ps(scope_p5, "axp", [P, TOK])
        t_ax = None
        for ch in range(NCH):
            w_list = [("LG", lg_asht)] if ch == 0 else []
            for nh in range(2):
                last = (ch == NCH - 1 and nh == 1)
                t_ax = pr.emit("tensor", lambda e, ch=ch, nh=nh: e.matmul(
                    axp[:, nh * 512:(nh + 1) * 512],
                    A_comb[:, ch * 128:(ch + 1) * 128],
                    big0[:, ch * TOK + nh * 512: ch * TOK + (nh + 1) * 512],
                    start=(ch == 0), stop=(ch == NCH - 1)),
                    waits=w_list if nh == 0 else [], inc="T" if last else None)
        sc_ax = pr.emit("scalar", lambda e: e.copy(Ax_sb[:, :], axp[:, :]),
                waits=[("T", t_ax)], inc="Sc")
        if tap_ax is not None:
            pr.emit("sync", lambda e: e.dma_start(tap_ax, Ax_sb[:, :]),
                    waits=[("Sc", sc_ax)], inc="LY", inc_n=16)
            pr.emit("sync", lambda e: e.dma_start(tap_acomb, A_comb[:, :]), inc="LY", inc_n=16)
            pr.emit("sync", lambda e: e.dma_start(tap_bstack, B_stack[:, :]), inc="LY", inc_n=16)
        pr.flush("p5_ax")
        pr.join_all()
        scope_p5.close()

        scope_p6 = ExitStack()
        stb_ps = ps(scope_p6, "stb_ps", [2, TOK])
        scope_wp6 = ExitStack()
        wp6 = [ps(scope_wp6, f"wp6{i}", [P, TOK]) for i in range(3)]

        def p6_resload(fc, waits):
            rs = f"R{fc % 2}"
            return (rs, pr.emit("sync", lambda e, fc=fc: e.dma_start(
                xres[fc % 2][:, :], y1buf[fc * P:(fc + 1) * P, :]),
                waits=list(waits) + [y1_marks[fc]], inc=rs, inc_n=16))

        def p6_deltas(fg):
            t_l = None
            first = True
            for fsub in range(2):
                fc = 2 * fg + fsub
                for half in range(2):
                    r0 = 0 if half == 0 else 64
                    w_list = [("LG", lg_bsht), ("LY", b_last_mark)] if (fg == 0 and first) else []
                    first = False
                    last = (fsub == 1 and half == 1)
                    t_l = pr.emit("tensor", lambda e, fc=fc, fsub=fsub, half=half, r0=r0: e.matmul(
                        wp6[fc % 3][:, half * 512:(half + 1) * 512],
                        B_stack[r0:r0 + 64, fc * P:(fc + 1) * P],
                        Ax_sb[r0:r0 + 64, half * 512:(half + 1) * 512],
                        start=False, stop=True),
                        waits=w_list, inc="T" if last else None)
            return t_l

        z2_marks = gemm_ln_phase(ws_f, bi_s, z2buf, p6_resload, p6_deltas, wp6, stb_ps)
        pr.flush("p6_ffn")
        pr.join_all()
        scope_wp6.close()

        rbc7 = ps(scope_p6, "rbc7", [P, TOK])
        v_fin2 = ln_finalize(stb_ps, rbc7)
        out_marks = ln_normalize(z2buf, z2_marks, False, outT, v_fin2)
        for ms in ("M0", "M1", "M2"):
            pr.wait("sync", ms, pr.c[ms])
        pr.flush("p7_out")
        scope_p6.close()
        scope_p37.close()
        scope_big.close()

    return nc



def _pretile(w):
    nout = w.shape[1]
    return np.ascontiguousarray(
        w.reshape(32, 128, nout // 256, 256).transpose(2, 1, 0, 3))


def _host_inputs(L0_attn_hidden_states, L0_attn_Wq, L0_attn_bq, L0_attn_Wk, L0_attn_bk,
                 L0_attn_Wv, L0_attn_bv, L0_attn_Wo, L0_attn_bo, L0_ffn_gate,
                 L0_ffn_kernel, L0_ffn_bias, L0_ffn_lora_A, L0_ffn_lora_B,
                 L0_ffn_lora_A_shared, L0_ffn_lora_B_shared):
    f32 = np.float32
    hid = np.asarray(L0_attn_hidden_states, f32)
    shared = {
        "wq_t": _pretile(np.asarray(L0_attn_Wq, f32)),
        "wk_t": _pretile(np.asarray(L0_attn_Wk, f32)),
        "wv_t": _pretile(np.asarray(L0_attn_Wv, f32)),
        "wo_t": _pretile(np.asarray(L0_attn_Wo, f32)),
        "kn_t": _pretile(np.asarray(L0_ffn_kernel, f32)),
        "bq_r": np.ascontiguousarray(np.asarray(L0_attn_bq, f32).reshape(32, 128).T),
        "bk_r": np.ascontiguousarray(np.asarray(L0_attn_bk, f32).reshape(8, 128).T),
        "bv_r": np.ascontiguousarray(np.asarray(L0_attn_bv, f32).reshape(8, 128).T),
        "bo_r": np.ascontiguousarray(np.asarray(L0_attn_bo, f32).reshape(32, 128).T),
        "bi_r": np.ascontiguousarray(np.asarray(L0_ffn_bias, f32).reshape(32, 128).T),
        "la2": np.ascontiguousarray(np.asarray(L0_ffn_lora_A, f32).reshape(512, DM)),
        "lbt": np.ascontiguousarray(
            np.asarray(L0_ffn_lora_B, f32).transpose(0, 2, 1)).reshape(512, DM),
        "asht": np.ascontiguousarray(np.asarray(L0_ffn_lora_A_shared, f32).T),
        "bsht": np.ascontiguousarray(np.asarray(L0_ffn_lora_B_shared, f32).T),
    }
    esel = np.zeros((16, 512), f32)
    for a in range(4):
        for pp in range(128):
            esel[4 * a + pp // 32, a * 128 + pp] = 1.0
    i32s = np.zeros((128, 32), f32)
    for pp in range(128):
        i32s[pp, pp % 32] = 1.0
    shared["esel_h"] = esel
    shared["i32s_h"] = i32s.astype(BF)
    shared["i128b_h"] = np.eye(128).astype(BF)
    shared["i128f_h"] = np.eye(128, dtype=f32)

    inv_freq = (1.0 / (10000.0 ** (np.arange(64, dtype=np.float64) / 64.0))).astype(f32)

    def dup_trig(ang, scale):
        c = np.cos(ang) * scale
        s = np.sin(ang) * scale
        C = np.concatenate([c, c], axis=0)
        Ss = np.concatenate([-s, s], axis=0)
        return C.astype(BF), Ss.astype(BF)



    tri = (np.arange(P)[:, None] <= np.arange(P)[None, :]).astype(f32)

    gate = np.asarray(L0_ffn_gate, f32)
    gT = np.ascontiguousarray(gate.T)

    in_maps = []
    for core in range(8):
        b, p = core // 2, core % 2
        xTb = np.ascontiguousarray(hid[b].T)
        own_cols = np.concatenate([np.arange((2 * j + p) * 128, (2 * j + p) * 128 + 128)
                                   for j in range(8)])
        xsel_a = np.ascontiguousarray(xTb[:, own_cols])
        pos_q = own_cols.astype(f32)
        ang_q = inv_freq[:, None] * pos_q[None, :]
        scale = 1.0 / math.sqrt(D)
        mk = np.zeros((NKC, P, P), f32)
        for kc in range(NKC):
            if p == 0:
                mk[kc] = tri if kc % 2 == 0 else 0.0
            else:
                mk[kc] = 1.0 if kc % 2 == 0 else tri
        m = dict(shared)
        m["xsel"] = xsel_a
        m["cq_h"], m["sq_h"] = dup_trig(ang_q, scale)
        m["ckq_h"], m["skq_h"] = dup_trig(ang_q, 1.0)
        m["mk_h"] = mk.astype(BF)
        m["g2"] = np.ascontiguousarray(gT[:, [2 * b, 2 * b + 1]])
        in_maps.append(m)
    return in_maps


def kernel(**inputs):
    in_maps = _host_inputs(**inputs)
    nc = build_nc()
    res = run_bass_kernel_spmd(nc, in_maps, core_ids=list(range(8)))
    out = np.empty((B, S, DM), np.float32)
    for core in range(8):
        b, p = core // 2, core % 2
        y = res.results[core]["outT"]
        yt = np.ascontiguousarray(y.T).reshape(8, 128, DM)
        out[b].reshape(16, 128, DM)[p::2] = yt
    return out
